# version 14
# baseline (speedup 1.0000x reference)
import numpy as np
import ml_dtypes
import concourse.bass as bass
import concourse.mybir as mybir
from concourse.bass_utils import run_bass_kernel_spmd

F32 = mybir.dt.float32
BF16 = mybir.dt.bfloat16
AF = mybir.ActivationFunctionType
ALU = mybir.AluOpType
AX = mybir.AxisListType

D = 1024
NH, HD = 16, 64
HT, RT = 3, 16
NT = HT + RT
TLOC, TREAL = NT * 128, RT * 128
NSLOT, TOPK, HFF = 64, 8, 512
N3 = 3 * D
LN_EPS = 1e-5
AR_SZ = 38 * 1024
FA_SZ = 6400


class KB:
    def __init__(self, taps=()):
        self.taps = set(taps)
        self.tap_handles = {}
        nc = self.nc = bass.Bass("TRN2", target_bir_lowering=False)
        self._ctx = []

        def din(name, shape):
            return nc.dram_tensor(name, shape, F32, kind="ExternalInput")

        def scrd(name, shape, dt):
            return nc.dram_tensor(name, shape, dt, kind="Internal")

        self.inp = {}
        for nm, shape in [
            ("x_slice", [TLOC, D]), ("halo_gate", [128, 1]),
            ("Wq", [D, D]), ("Wk", [D, D]), ("Wv", [D, D]),
            ("Wa", [D, NH]), ("Wb", [D, NH]), ("ba", [1, NH]), ("bb", [1, NH]),
            ("Wg", [D, D]), ("Wo", [D, D]), ("g_norm", [1, D]),
            ("slot_init", [NSLOT, D]), ("slot_scale", [NSLOT, 1]),
            ("w1", [HFF, 2 * D]), ("b1", [1, HFF]),
            ("w2", [NSLOT, HFF]), ("b2", [1, NSLOT]),
            ("gru_wih", [N3, D]), ("gru_whh", [N3, D]),
            ("gru_bih", [1, N3]), ("gru_bhh", [1, N3]),
            ("wv", [D, D]), ("bv", [1, D]), ("wo", [D, D]), ("bo", [1, D]),
            ("tau", [1, 1]),
        ]:
            self.inp[nm] = din(nm, shape)
        self.out = nc.dram_tensor("out", [TREAL, D], F32, kind="ExternalOutput")

        self.xn_dram = scrd("xn_dram", [TLOC, D], BF16)
        self.q_dram = scrd("q_dram", [TLOC, D], BF16)
        self.k_dram = scrd("k_dram", [TLOC, D], BF16)
        self.v_dram = scrd("v_dram", [TLOC, D], BF16)
        self.gate_dram = scrd("gate_dram", [TLOC, D], BF16)
        self.yf_dram = scrd("yf_dram", [TLOC, D], BF16)
        self.al_dram = scrd("al_dram", [TLOC, 128], BF16)
        self.wbf = {}
        for nm, shape in [("w1", [HFF, 2 * D]), ("w2", [NSLOT, HFF]),
                          ("wih", [N3, D]), ("whh", [N3, D]),
                          ("wv", [D, D]), ("wo", [D, D]), ("woa", [D, D]),
                          ("wg", [D, D]), ("wq", [D, D]), ("wk", [D, D]),
                          ("wvp", [D, D]), ("wab", [D, 2 * NH])]:
            self.wbf[nm] = scrd(nm + "_bfd", shape, BF16)
        self.gh_dram = scrd("gh_dram", [NSLOT, N3], BF16)
        self.cc_in = scrd("cc_in", [NSLOT, D + 16], F32)
        self.cc_out = scrd("cc_out", [NSLOT, D + 16], F32)
        self.lgc_dram = scrd("lgc_dram", [NT, NH], F32)

        sb = self._sb
        self.x_tm = sb("x_tm", [128, NT * D], F32)
        self.arena = sb("arena", [128, AR_SZ], BF16)
        self.farena = sb("farena", [128, FA_SZ], F32)
        self.ident_f = sb("ident_f", [128, 128], F32)
        self.ident_b = sb("ident_b", [128, 128], BF16)
        self.mask_incl = sb("mask_incl", [128, 128], F32)
        self.mask_strict = sb("mask_strict", [128, 128], F32)
        self.ones_col = sb("ones_col", [128, 1], BF16)
        self.ones_row = sb("ones_row", [1, 128], BF16)
        self.onesf_row = sb("onesf_row", [1, 128], F32)
        self.gate_col = sb("gate_col", [128, 1], F32)
        self.gateneg_col = sb("gateneg_col", [128, 1], F32)
        self.stats = sb("stats", [128, 128], F32)
        self.biasrows = sb("biasrows", [1, 3200], BF16)
        self.effb1_col = sb("effb1_col", [128, 4], F32)
        self.stau_col = sb("stau_col", [128, 1], F32)
        self.hs_slot = sb("hs_slot", [64, D], F32)
        self.hs_bf = sb("hs_bf", [64, D], BF16)
        self.smean_bf = sb("smean_bf", [128, 8], BF16)
        self.ab_tm = sb("ab_tm", [128, NT * 3 * NH], F32)
        self.lg_tm = sb("lg_tm", [128, NT * NH], F32)
        self.lgc_flat = sb("lgc_flat", [1, NT * NH], F32)
        self.M_all = sb("M_all", [128, (NH // 2) * HD], F32)
        self.M_bf = sb("M_bf", [128, (NH // 2) * HD], BF16)
        self.fmrow = sb("fmrow", [16, 256], F32)
        self.cvals = sb("cvals", [128, 16], F32)
        self._consts = [LN_EPS, -0.5, -1.0, 1.0, 0.5]
        for i, v in enumerate(self._consts):
            self.nc.const_aps.register(F32, v, self.cvals[:, i:i + 1])

        psf = lambda n: self._ps(n, [128, 512], F32)
        self.pA, self.pB, self.pC, self.pD = psf("pA"), psf("pB"), psf("pC"), psf("pD")
        self.pE, self.pF, self.pG = psf("pE"), psf("pF"), psf("pG")
        self.pH = self._ps("pH", [128, 1024], BF16)

        self.sem = {}
        for nm in ["sL", "sG", "sP", "sV", "sA", "sO", "sCC", "sScl"]:
            cm = nc.semaphore(nm)
            h = cm.__enter__()
            self._ctx.append(cm)
            self.sem[nm] = [h, 0]

    def _sb(self, name, shape, dt):
        cm = self.nc.sbuf_tensor(name, shape, dt)
        t = cm.__enter__()
        self._ctx.append(cm)
        return t

    def _ps(self, name, shape, dt):
        cm = self.nc.psum_tensor(name, shape, dt)
        t = cm.__enter__()
        self._ctx.append(cm)
        return t

    def inc(self, instr, sem, n=1):
        instr.then_inc(self.sem[sem][0], n)
        self.sem[sem][1] += n
        return instr

    def cnt(self, sem):
        return self.sem[sem][1]

    def wait(self, eng, sem, val=None):
        eng.wait_ge(self.sem[sem][0], self.cnt(sem) if val is None else val)

    def block(self, **bodies):
        with self.nc.Block() as blk:
            for en, fn in bodies.items():
                getattr(blk, en)(fn)

    def barrier(self):
        k = self
        self.block(tensor=lambda e: k.inc(e.nop(), "sP"),
                   vector=lambda e: k.inc(e.engine_nop(), "sV"),
                   scalar=lambda e: k.inc(e.nop(), "sA"),
                   gpsimd=lambda e: k.inc(e.engine_nop(), "sO"),
                   sync=lambda e: k.inc(e.nop(), "sScl"))
        tgt = {nm: self.cnt(nm) for nm in
               ["sP", "sV", "sA", "sO", "sScl", "sL", "sG", "sCC"]}

        def wall(e):
            for nm, c in tgt.items():
                e.wait_ge(self.sem[nm][0], c)
        self.block(tensor=wall, vector=wall, scalar=wall, gpsimd=wall, sync=wall)

    tap_shapes = {
        "xn1": ([TLOC, D], BF16), "h1": ([HFF, TLOC], BF16),
        "alpha1": ([TLOC, 128], BF16), "slotred1": ([NSLOT, D + 16], F32),
        "sv1": ([NSLOT, D], BF16), "x1": ([TLOC, D], F32),
        "qn": ([TLOC, D], BF16), "kn": ([TLOC, D], BF16),
        "vv": ([TLOC, D], BF16), "lgtm": ([128, NT * NH], F32),
        "yf": ([TLOC, D], BF16), "x2": ([TLOC, D], F32),
        "abtm": ([128, NT * 3 * NH], F32),
        "dbgab": ([128, 320], F32),
        "dbg": ([128, 320], F32),
        "dbg2": ([128, 8 * TLOC], BF16),
        "dbg3": ([128, 16 * 512], BF16),
    }

    def tapout(self, name, emit_fn):
        if name not in self.taps or name in self.tap_handles:
            return
        h = self.nc.dram_tensor("tap_" + name, self.tap_shapes[name][0],
                                self.tap_shapes[name][1], kind="ExternalOutput")
        self.tap_handles[name] = h
        self.barrier()
        emit_fn(h)
        self.barrier()

    def tap_dram(self, name, src):
        self.tapout(name, lambda hnd: self.block(
            gpsimd=lambda g: self.inc(g.dma_start(hnd.ap(), src.ap()), "sG", 16)))

    def tap_xtm(self, name, tiles):
        def emit(hnd):
            def g(g_):
                for i, t in enumerate(tiles):
                    self.inc(g_.dma_start(
                        bass.AP(hnd, i * 128 * D, [[D, 128], [1, D]]),
                        self.x_tm[:, t * D:(t + 1) * D]), "sG", 16)
            self.block(gpsimd=g)
        self.tapout(name, emit)

    def boot(self):
        k = self
        inp = self.inp
        FA, A = self.farena, self.arena
        wld = FA[:, 0:4096]
        wcast = A[:, 0:4096]

        def s_body(s):
            for t in range(NT):
                k.inc(s.dma_start(
                    k.x_tm[:, t * D:(t + 1) * D],
                    bass.AP(inp["x_slice"], t * 128 * D, [[D, 128], [1, D]])),
                    "sL", 16)
            k.inc(s.dma_start(k.gate_col[:], inp["halo_gate"].ap()), "sL", 16)

        def g_body(g):
            for i, v in enumerate(self._consts):
                g.memset(k.cvals[:, i:i + 1], v)
            k.inc(g.memset(k.ident_f[:], 0.0), "sO")
            k.wait(g, "sO")
            k.inc(g.affine_select(out=k.ident_f[:], in_=k.ident_f[:],
                                  compare_op=ALU.not_equal, fill=1.0, base=0,
                                  pattern=[[-1, 128]], channel_multiplier=1),
                  "sO")
            k.wait(g, "sO")
            g.tensor_copy(k.ident_b[:], k.ident_f[:])
            g.memset(k.ones_col[:], 1.0)
            g.memset(k.ones_row[:], 1.0)
            g.memset(k.onesf_row[:], 1.0)
            g.memset(k.M_all[:], 0.0)
            g.memset(k.M_bf[:], 0.0)
            g.iota(k.mask_incl[:], [[1, 128]], channel_multiplier=0,
                   allow_small_or_imprecise_dtypes=True)
            k.inc(g.iota(k.mask_strict[:], [[0, 128]], channel_multiplier=1,
                         allow_small_or_imprecise_dtypes=True), "sO")
            k.wait(g, "sO")
            k.inc(g.tensor_tensor(k.mask_incl[:], k.mask_incl[:],
                                  k.mask_strict[:], ALU.subtract), "sO")
            k.wait(g, "sO")
            k.inc(g.tensor_single_scalar(k.mask_strict[:], k.mask_incl[:], -0.5,
                                         ALU.is_gt), "sO")
            k.wait(g, "sO")
            k.inc(g.tensor_scalar_mul(k.mask_strict[:], k.mask_strict[:],
                                      -1e30), "sO")
            k.wait(g, "sO")
            k.inc(g.tensor_single_scalar(k.mask_incl[:], k.mask_incl[:], 0.5,
                                         ALU.is_gt), "sO")
            k.wait(g, "sO")
            k.inc(g.tensor_scalar_mul(k.mask_incl[:], k.mask_incl[:], -1e30),
                  "sO")
        self.block(sync=s_body, gpsimd=g_body)

        jobs = [
            ("w1", inp["w1"], HFF, 2 * D, None),
            ("w2", inp["w2"], NSLOT, HFF, None),
            ("wq", inp["Wq"], D, D, None),
            ("wk", inp["Wk"], D, D, None),
            ("wvp", inp["Wv"], D, D, None),
            ("wg", inp["Wg"], D, D, None),
            ("woa", inp["Wo"], D, D, None),
            ("wih", inp["gru_wih"], N3, D, None),
            ("whh", inp["gru_whh"], N3, D, None),
            ("wv", inp["wv"], D, D, None),
            ("wo", inp["wo"], D, D, 0.5),
        ]
        chunks = []
        for nm, src, rows, cols, scale in jobs:
            dram = self.wbf[nm]
            nch = (rows + 127) // 128
            for kc in range(nch):
                chunks.append((src, dram, kc, min(128, rows - kc * 128),
                               cols, scale))
        vc = []
        nchk = len(chunks)
        for i in range(nchk + 1):
            if i < nchk:
                src, dram, kc, rr, cols, scale = chunks[i]
                p = i % 2
                wl = FA[:, p * 2048:p * 2048 + 2048]
                wc = A[:, p * 2048:p * 2048 + 2048]

                def g1(g, i=i, src=src, kc=kc, rr=rr, cols=cols, wl=wl):
                    if i >= 2:
                        k.wait(g, "sV", vc[i - 2])
                    k.inc(g.dma_start(
                        wl[0:rr, 0:cols],
                        bass.AP(src, kc * 128 * cols, [[cols, rr], [1, cols]])),
                        "sG", 16)

                def v1(v, rr=rr, cols=cols, scale=scale, wl=wl, wc=wc):
                    k.wait(v, "sG")
                    if scale is None:
                        i_ = v.tensor_copy(wc[0:rr, 0:cols], wl[0:rr, 0:cols])
                    else:
                        i_ = v.tensor_scalar_mul(wc[0:rr, 0:cols],
                                                 wl[0:rr, 0:cols], scale)
                    k.inc(i_, "sV")
                self.block(gpsimd=g1, vector=v1)
                vc.append(self.cnt("sV"))
            if i >= 1:
                src, dram, kc, rr, cols, scale = chunks[i - 1]
                p = (i - 1) % 2
                wc = A[:, p * 2048:p * 2048 + 2048]

                def g2(g, i=i, dram=dram, kc=kc, rr=rr, cols=cols, wc=wc):
                    k.wait(g, "sV", vc[i - 1])
                    k.inc(g.dma_start(
                        bass.AP(dram, kc * 128 * cols, [[cols, rr], [1, cols]]),
                        wc[0:rr, 0:cols]), "sG", 16)
                self.block(gpsimd=g2)
        for kc in range(8):
            def g1(g, kc=kc):
                k.wait(g, "sV")
                k.inc(g.dma_start(
                    wld[:, 0:NH],
                    bass.AP(inp["Wa"], kc * 128 * NH, [[NH, 128], [1, NH]])),
                    "sG", 16)
                k.inc(g.dma_start(
                    wld[:, NH:2 * NH],
                    bass.AP(inp["Wb"], kc * 128 * NH, [[NH, 128], [1, NH]])),
                    "sG", 16)

            def v1(v):
                k.wait(v, "sG")
                k.inc(v.tensor_copy(wcast[:, 0:2 * NH], wld[:, 0:2 * NH]), "sV")

            def g2(g, kc=kc):
                k.wait(g, "sV")
                k.inc(g.dma_start(
                    bass.AP(self.wbf["wab"], kc * 128 * 2 * NH,
                            [[2 * NH, 128], [1, 2 * NH]]),
                    wcast[:, 0:2 * NH]), "sG", 16)
            self.block(gpsimd=g1, vector=v1)
            self.block(gpsimd=g2)
        self.barrier()

        self.BR = {}
        off = 0
        for nm, ln in [("b2", NSLOT), ("bv", D), ("halfbo", D), ("ba", NH),
                       ("bb", NH), ("gnorm", D), ("stau", 16)]:
            self.BR[nm] = (off, ln)
            off += ln
        assert off <= 3200
        rowf = FA[0:1, 0:3200]

        def g3(g):
            k.inc(g.memset(rowf[0:1, :], 0.0), "sO")
            k.wait(g, "sO")
            for nm, src in [("b2", inp["b2"]), ("bv", inp["bv"]),
                            ("halfbo", inp["bo"]), ("ba", inp["ba"]),
                            ("bb", inp["bb"]), ("gnorm", inp["g_norm"])]:
                o, ln = self.BR[nm]
                k.inc(g.dma_start(rowf[0:1, o:o + ln], src.ap()), "sG", 16)
            o = self.BR["stau"][0]
            k.inc(g.dma_start(rowf[0:1, o:o + 1], inp["tau"].ap()), "sG", 16)
            k.inc(g.dma_start(FA[0:64, 4096:4096 + D], inp["slot_init"].ap()),
                  "sG", 16)
            k.inc(g.dma_start(FA[0:64, 5120:5121],
                              bass.AP(inp["slot_scale"], 0, [[1, 64], [1, 1]])),
                  "sG", 16)

        def a3x(a):
            k.wait(a, "sG")
            o = self.BR["stau"][0]
            a.mul(rowf[0:1, o + 4:o + 5], rowf[0:1, o:o + 1], -1.0)
            k.inc(a.nop(), "sA")

        def v3(v):
            k.wait(v, "sA")
            o = self.BR["stau"][0]
            k.inc(v.tensor_tensor(rowf[0:1, o + 1:o + 2], rowf[0:1, o:o + 1],
                                  rowf[0:1, o + 4:o + 5], ALU.max), "sV")
            o = self.BR["halfbo"][0]
            v.tensor_scalar_mul(rowf[0:1, o:o + D], rowf[0:1, o:o + D], 0.5)
            o = self.BR["stau"][0]
            k.wait(v, "sV")
            k.inc(v.tensor_scalar_add(rowf[0:1, o + 2:o + 3],
                                      rowf[0:1, o + 1:o + 2], 0.1), "sV")
            k.wait(v, "sV")
            k.inc(v.reciprocal(rowf[0:1, o + 3:o + 4], rowf[0:1, o + 2:o + 3]),
                  "sV")
            k.wait(v, "sV")
            v.tensor_copy(k.biasrows[0:1, :], rowf[0:1, :])
            k.inc(v.tensor_scalar(k.hs_slot[:], FA[0:64, 4096:4096 + D],
                                  FA[0:64, 5120:5121], None, ALU.mult), "sV")
            k.wait(v, "sV")
            k.inc(v.tensor_copy(k.hs_bf[:], k.hs_slot[:]), "sV")
        self.block(gpsimd=g3, scalar=a3x)
        self.block(vector=v3)
        self.barrier()

        o = self.BR["stau"][0]

        def t4(t):
            k.inc(t.matmul(k.pA[:, 0:1], k.ones_row[0:1, :],
                           k.biasrows[0:1, o + 3:o + 4], start=True, stop=True),
                  "sP")
            for kc in range(8):
                k.inc(t.matmul(k.pB[:, kc:kc + 1],
                               k.hs_bf[:, kc * 128:(kc + 1) * 128],
                               k.ones_col[0:64, :], start=True, stop=True), "sP")

        def v4(v):
            k.wait(v, "sP")
            v.tensor_scalar_mul(k.gateneg_col[:], k.gate_col[:], -1.0)
            v.tensor_copy(k.stau_col[:], k.pA[:, 0:1])
            k.inc(v.tensor_scalar_mul(k.smean_bf[:], k.pB[:, 0:8], 1.0 / NSLOT),
                  "sV")
        self.block(tensor=t4)
        self.block(vector=v4)
        self.barrier()

    def gh_precompute(self):
        k = self
        A, FA = self.arena, self.farena
        HSF = 8192
        BH = HSF + 512
        GRUB = BH + N3 + 64

        def t1(t):
            for kc in range(8):
                t.transpose(k.pH[:, kc * 64:(kc + 1) * 64],
                            k.hs_bf[:, kc * 128:(kc + 1) * 128],
                            k.ident_b[0:64, 0:64])
            k.inc(t.nop(), "sP")

        def v1(v):
            k.wait(v, "sP")
            k.inc(v.tensor_copy(A[:, HSF:HSF + 512], k.pH[:, 0:512]), "sV")
        self.block(tensor=t1)
        self.block(vector=v1)

        def g1(g):
            k.inc(g.dma_start(FA[0:1, 0:N3], self.inp["gru_bhh"].ap()), "sG", 16)

        def v2(v):
            k.wait(v, "sG")
            k.inc(v.tensor_copy(A[0:1, BH:BH + N3], FA[0:1, 0:N3]), "sV")
        self.block(gpsimd=g1, vector=v2)

        BANK6 = [k.pA, k.pB, k.pC, k.pD, k.pE, k.pF]
        for kc in range(8):
            def s1(s, kc=kc):
                if kc >= 2:
                    k.wait(s, "sP", self._gh_p0 + (kc - 1))
                k.inc(s.dma_start(
                    A[:, GRUB + (kc % 2) * N3:GRUB + (kc % 2) * N3 + N3],
                    bass.AP(self.wbf["whh"], kc * 128, [[D, N3], [1, 128]]),
                    transpose=True), "sL", 16)
            self.block(sync=s1)

            def t2(t, kc=kc):
                k.wait(t, "sL")
                if kc == 0:
                    k.wait(t, "sV")
                for n3c in range(6):
                    i = t.matmul(
                        BANK6[n3c][0:64, 0:512],
                        A[:, HSF + kc * 64:HSF + (kc + 1) * 64],
                        A[:, GRUB + (kc % 2) * N3 + n3c * 512:
                          GRUB + (kc % 2) * N3 + (n3c + 1) * 512],
                        start=(kc == 0), stop=False)
                if kc == 7:
                    for n3c in range(6):
                        i = t.matmul(
                            BANK6[n3c][0:64, 0:512], k.ones_row[0:1, 0:64],
                            A[0:1, BH + n3c * 512:BH + (n3c + 1) * 512],
                            start=False, stop=True)
                k.inc(i, "sP")
            if kc == 0:
                self._gh_p0 = self.cnt("sP")
            self.block(tensor=t2)

        def v3(v):
            k.wait(v, "sP")
            for n3c in range(6):
                i = v.tensor_copy(A[0:64, HSF + n3c * 512:HSF + (n3c + 1) * 512],
                                  BANK6[n3c][0:64, 0:512])
            k.inc(i, "sV")

        def g3(g):
            k.wait(g, "sV")
            k.inc(g.dma_start(self.gh_dram.ap(), A[0:64, HSF:HSF + N3]), "sG", 16)
        self.block(vector=v3)
        self.block(gpsimd=g3)
        self.barrier()

    def ln(self, tiles):
        k = self
        ntl = len(tiles)
        ssum = k.stats[:, 0:ntl]
        ssq = k.stats[:, 24:24 + ntl]
        mu = k.stats[:, 48:48 + ntl]
        tmp = k.stats[:, 72:72 + ntl]
        rstd = k.stats[:, 96:96 + ntl]
        sqs = k.farena[:, 0:D]
        xnb = [k.arena[:, 4096 + i * D:4096 + (i + 1) * D] for i in range(3)]

        def v1(v):
            for j, t in enumerate(tiles):
                v.reduce_sum(ssum[:, j:j + 1], k.x_tm[:, t * D:(t + 1) * D],
                             axis=AX.X)
            k.inc(v.engine_nop(), "sV")

        def a1(a):
            for j, t in enumerate(tiles):
                k.inc(a.activation(sqs[:], k.x_tm[:, t * D:(t + 1) * D],
                                   AF.Square, accum_out=ssq[:, j:j + 1]), "sA")
                k.wait(a, "sA")
            k.inc(a.nop(), "sA")
        self.block(vector=v1, scalar=a1)

        def v2(v):
            k.wait(v, "sA")
            k.inc(v.tensor_scalar_mul(mu[:], ssum[:], 1.0 / D), "sV")
        self.block(vector=v2)

        def a2(a):
            k.wait(a, "sV")
            k.inc(a.activation(tmp[:], mu[:], AF.Square), "sA")
        self.block(scalar=a2)

        def v3(v):
            k.wait(v, "sA")
            k.inc(v.scalar_tensor_tensor(rstd[:], ssq[:], 1.0 / D, tmp[:],
                                         ALU.mult, ALU.subtract), "sV")
        self.block(vector=v3)

        def a3(a):
            k.wait(a, "sV")
            k.inc(a.activation(tmp[:], rstd[:], AF.Ln, bias=LN_EPS), "sA")
            k.wait(a, "sA")
            k.inc(a.activation(rstd[:], tmp[:], AF.Exp, scale=-0.5), "sA")
        self.block(scalar=a3)

        g0 = [None]
        for j, t in enumerate(tiles):
            buf = xnb[j % 3]

            def v4(v, j=j, t=t, buf=buf):
                if j == 0:
                    k.wait(v, "sA")
                if j >= 3:
                    k.wait(v, "sG", g0[0] + 16 * (j - 2))
                k.inc(v.tensor_scalar(buf, k.x_tm[:, t * D:(t + 1) * D],
                                      mu[:, j:j + 1], rstd[:, j:j + 1],
                                      ALU.subtract, ALU.mult), "sV")

            def g4(g, t=t, buf=buf):
                k.wait(g, "sV")
                k.inc(g.dma_start(
                    bass.AP(self.xn_dram, t * 128 * D, [[D, 128], [1, D]]),
                    buf), "sG", 16)
            if j == 0:
                g0[0] = self.cnt("sG")
            self.block(vector=v4, gpsimd=g4)
        self.barrier()

    def xnfm_load(self, src_dram, dst_off, rows=TLOC):
        k = self

        def s1(s):
            for kc in range(8):
                k.inc(s.dma_start(
                    k.arena[:, dst_off + kc * rows:dst_off + (kc + 1) * rows],
                    bass.AP(src_dram, kc * 128, [[D, rows], [1, 128]]),
                    transpose=True), "sL", 16)
        self.block(sync=s1)

    def ffn(self, tiles, tag):
        k = self
        A, FA = self.arena, self.farena
        rows = TLOC
        real = [t for t in tiles if t >= HT]
        tcols = [(i * 512, min(512, rows - i * 512))
                 for i in range((rows + 511) // 512)]

        XN = 0
        W1T = XN + 8 * rows
        W2T = W1T + 8192
        HFM = W2T + 256
        XNT = 0
        ALB = XNT + 3 * 1032
        assert HFM + 4 * rows <= AR_SZ

        def s1(s):
            for kc in range(8):
                k.inc(s.dma_start(
                    A[:, XN + kc * rows:XN + (kc + 1) * rows],
                    bass.AP(self.xn_dram, kc * 128, [[D, rows], [1, 128]]),
                    transpose=True), "sL", 16)
            for kc in range(16):
                k.inc(s.dma_start(
                    A[:, W1T + kc * 512:W1T + (kc + 1) * 512],
                    bass.AP(self.wbf["w1"], kc * 128, [[2 * D, HFF], [1, 128]]),
                    transpose=True), "sL", 16)
            for kc in range(4):
                k.inc(s.dma_start(
                    A[:, W2T + kc * 64:W2T + (kc + 1) * 64],
                    bass.AP(self.wbf["w2"], kc * 128, [[HFF, NSLOT], [1, 128]]),
                    transpose=True), "sL", 16)
        self.block(sync=s1)

        if tag == 1:
            def emit_dbg2(hnd):
                self.block(gpsimd=lambda g: self.inc(g.dma_start(
                    bass.AP(hnd, 0, [[8 * rows, 128], [1, 8 * rows]]),
                    A[:, XN:XN + 8 * rows]), "sG", 16))
            self.tapout("dbg2", emit_dbg2)
            def emit_dbg3(hnd):
                self.block(gpsimd=lambda g: self.inc(g.dma_start(
                    bass.AP(hnd, 0, [[16 * 512, 128], [1, 16 * 512]]),
                    A[:, W1T:W1T + 16 * 512]), "sG", 16))
            self.tapout("dbg3", emit_dbg3)

        def g1(g):
            k.inc(g.dma_start(FA[0:1, 0:HFF], self.inp["b1"].ap()), "sG", 16)
        self.block(gpsimd=g1)

        def t1(t):
            k.wait(t, "sL")
            k.wait(t, "sG")
            for hc in range(4):
                for kc in range(8):
                    t.matmul(k.pE[:, hc:hc + 1],
                             A[:, W1T + (8 + kc) * 512 + hc * 128:
                               W1T + (8 + kc) * 512 + (hc + 1) * 128],
                             k.smean_bf[:, kc:kc + 1],
                             start=(kc == 0), stop=(kc == 7))
            for hc in range(4):
                t.transpose(k.pF[:, hc:hc + 1], FA[0:1, hc * 128:(hc + 1) * 128],
                            k.ident_f[0:1, 0:1])
            k.inc(t.nop(), "sP")
        self.block(tensor=t1)

        def a1(a):
            k.wait(a, "sP")
            k.inc(a.copy(FA[:, 208:212], k.pF[:, 0:4]), "sA")
        self.block(scalar=a1)

        def v1(v):
            k.wait(v, "sA")
            k.inc(v.tensor_tensor(k.effb1_col[:], k.pE[:, 0:4], FA[:, 208:212],
                                  ALU.add), "sV")
        self.block(vector=v1)
        self.barrier()

        pAB = [k.pA, k.pB]
        nmm = 0
        a0 = [None]
        for hc in range(4):
            for (c0, cw) in tcols:
                bank = pAB[nmm % 2]

                def tmm(t, hc=hc, c0=c0, cw=cw, bank=bank, nmm=nmm):
                    if nmm >= 2:
                        k.wait(t, "sA", a0[0] + (nmm - 1))
                    for kc in range(8):
                        i = t.matmul(
                            bank[:, 0:cw],
                            A[:, W1T + kc * 512 + hc * 128:
                              W1T + kc * 512 + (hc + 1) * 128],
                            A[:, XN + kc * rows + c0:XN + kc * rows + c0 + cw],
                            start=(kc == 0), stop=(kc == 7))
                    k.inc(i, "sP")

                def act(a, hc=hc, c0=c0, cw=cw, bank=bank):
                    k.wait(a, "sP")
                    k.inc(a.activation(
                        A[:, HFM + hc * rows + c0:HFM + hc * rows + c0 + cw],
                        bank[:, 0:cw], AF.Gelu,
                        bias=k.effb1_col[:, hc:hc + 1]), "sA")
                if nmm == 0:
                    a0[0] = self.cnt("sA")
                self.block(tensor=tmm, scalar=act)
                nmm += 1
        self.barrier()
        if tag == 1:
            def emit_h1(hnd):
                def g(g_):
                    for hc in range(4):
                        self.inc(g_.dma_start(
                            bass.AP(hnd, hc * 128 * rows, [[rows, 128], [1, rows]]),
                            A[:, HFM + hc * rows:HFM + (hc + 1) * rows]),
                            "sG", 16)
                self.block(gpsimd=g)
            self.tapout("h1", emit_h1)

        o_b2 = self.BR["b2"][0]
        m8 = FA[:, 0:8]
        mneg = FA[:, 8:9]
        den = FA[:, 9:10]
        rden = FA[:, 10:11]
        maskt = FA[:, 16:80]
        et = FA[:, 80:144]
        lgsb = FA[:, 144:208]
        aun = FA[:, 256:320]
        pool_first = [True]
        aexp = []
        vfin = []
        xld = {}
        pcnt = []

        for j, t in enumerate(tiles):
            xbuf = A[:, XNT + (j % 3) * 1032:XNT + (j % 3) * 1032 + 1032]
            if t in real:
                def s2(s, t=t, j=j, xbuf=xbuf):
                    ridx = real.index(t)
                    if ridx >= 3:
                        k.wait(s, "sP", pcnt[ridx - 3])
                    k.inc(s.dma_start(
                        xbuf[:, 0:D],
                        bass.AP(self.xn_dram, t * 128 * D, [[D, 128], [1, D]])),
                        "sL", 16)
                    xld[t] = self.cnt("sL")
                self.block(sync=s2)
                if len(xld) <= 3:
                    def g2(g, xbuf=xbuf):
                        g.memset(xbuf[:, 1025:1032], 0.0)
                        k.inc(g.memset(xbuf[:, 1024:1025], 1.0), "sO")
                    self.block(gpsimd=g2)

            def t3(t_, t=t, j=j):
                if j == 0:
                    k.wait(t_, "sO")
                if j >= 1:
                    k.wait(t_, "sA", aexp[j - 1])
                for hc in range(4):
                    t_.matmul(k.pC[:, 0:64],
                              A[:, HFM + hc * rows + t * 128:
                                HFM + hc * rows + (t + 1) * 128],
                              A[:, W2T + hc * 64:W2T + (hc + 1) * 64],
                              start=(hc == 0), stop=False)
                k.inc(t_.matmul(k.pC[:, 0:64], k.ones_row[0:1, :],
                                k.biasrows[0:1, o_b2:o_b2 + NSLOT],
                                start=False, stop=True), "sP")
            self.block(tensor=t3)

            def v3(v, j=j):
                k.wait(v, "sP")
                if j >= 1:
                    k.wait(v, "sA", aexp[j - 1])
                k.inc(v.tensor_scalar(lgsb[:], k.pC[:, 0:64],
                                      k.stau_col[:, 0:1], None, ALU.mult), "sV")
                k.wait(v, "sV")
                k.inc(v.max(m8[:], lgsb[:]), "sV")
                k.wait(v, "sV")
                v.tensor_scalar(mneg[:], m8[:, 0:1], -1.0, None, ALU.mult)
                v.tensor_scalar(maskt[:], lgsb[:], m8[:, 7:8], -1e30,
                                ALU.is_lt, ALU.mult)
                k.inc(v.tensor_tensor(aun[:], lgsb[:], maskt[:], ALU.add),
                      "sV")
            self.block(vector=v3)

            def a3(a):
                k.wait(a, "sV")
                k.inc(a.activation(et[:], aun[:], AF.Exp,
                                   bias=mneg[:, 0:1], accum_out=den[:]), "sA")
            self.block(scalar=a3)
            aexp.append(self.cnt("sA"))

            albuf = A[:, ALB + (j % 3) * 64:ALB + (j % 3) * 64 + 64]

            def v4(v, albuf=albuf, j=j):
                k.wait(v, "sA")
                if j >= 3:
                    k.wait(v, "sG", self._al_g[j - 3])
                    if tiles[j - 3] in real:
                        k.wait(v, "sP", pcnt[real.index(tiles[j - 3])])
                k.inc(v.reciprocal(rden[:], den[:]), "sV")
                k.wait(v, "sV")
                k.inc(v.tensor_scalar(albuf, et[:], rden[:, 0:1], None,
                                      ALU.mult), "sV")
            if j == 0:
                self._al_g = []
            self.block(vector=v4)
            vfin.append(self.cnt("sV"))
            if j == 1 and tag == 1 and "dbg" in self.taps:
                def emit_dbg(hnd):
                    def vv(v):
                        v.tensor_copy(FA[:, 12:13], k.stau_col[:])
                        o_st = self.BR["stau"][0]
                        v.tensor_copy(FA[0:1, 300:308],
                                      k.biasrows[0:1, o_st:o_st + 8])
                        k.inc(v.tensor_copy(FA[:, 13:14], k.effb1_col[:, 0:1]),
                              "sV")
                    self.block(vector=vv)
                    def gg(g):
                        k.wait(g, "sV")
                        self.inc(g.dma_start(
                            bass.AP(hnd, 0, [[320, 128], [1, 320]]),
                            FA[:, 0:320]), "sG", 16)
                    self.block(gpsimd=gg)
                self.tapout("dbg", emit_dbg)

            def g5(g, t=t, albuf=albuf):
                k.wait(g, "sV")
                k.inc(g.dma_start(
                    bass.AP(self.al_dram, t * 128 * 128, [[128, 128], [1, 64]]),
                    albuf), "sG", 16)
            self.block(gpsimd=g5)
            self._al_g.append(self.cnt("sG"))

            if t in real:
                xbuf = A[:, XNT + (j % 3) * 1032:XNT + (j % 3) * 1032 + 1032]
                first = pool_first[0]
                pool_first[0] = False

                def t5(t_, t=t, xbuf=xbuf, albuf=albuf, first=first):
                    k.wait(t_, "sV", vfin[-1])
                    k.wait(t_, "sL", xld[t])
                    k.wait(t_, "sO")
                    t_.matmul(k.pF[0:64, 0:512], albuf, xbuf[:, 0:512],
                              start=first, stop=False)
                    t_.matmul(k.pG[0:64, 0:512], albuf, xbuf[:, 512:1024],
                              start=first, stop=False)
                    k.inc(t_.matmul(k.pE[0:64, 8:16], albuf, xbuf[:, 1024:1032],
                                    start=first, stop=False), "sP")
                self.block(tensor=t5)
                pcnt.append(self.cnt("sP"))
            if getattr(self, "_paranoid", False):
                self.barrier()
        self.barrier()
        if tag == 1:
            self.tap_dram("alpha1", self.al_dram)

        slotbuf = FA[0:64, 0:D + 16]

        def v6(v):
            v.tensor_copy(slotbuf[:, 0:512], k.pF[0:64, 0:512])
            v.tensor_copy(slotbuf[:, 512:1024], k.pG[0:64, 0:512])
            k.inc(v.tensor_copy(slotbuf[:, 1024:1032], k.pE[0:64, 8:16]), "sV")
        self.block(vector=v6)

        def g6(g):
            k.wait(g, "sV")
            k.inc(g.dma_start(self.cc_in.ap(), slotbuf[:, 0:D + 16]), "sG", 16)
            k.wait(g, "sG")
            k.inc(g.collective_compute(
                "AllReduce", ALU.add, [[0, 1, 2, 3], [4, 5, 6, 7]],
                [self.cc_in.ap().opt()], [self.cc_out.ap().opt()]), "sCC")
            k.wait(g, "sCC")
            k.inc(g.dma_start(slotbuf[:, 0:D + 16], self.cc_out.ap()), "sG", 16)
        self.block(gpsimd=g6)
        self.barrier()
        if tag == 1:
            def emit_sr(hnd):
                self.block(gpsimd=lambda g: self.inc(
                    g.dma_start(hnd.ap(), slotbuf[:, 0:D + 16]), "sG", 16))
            self.tapout("slotred1", emit_sr)

        GRU = 0
        GIF = GRU + 2 * N3
        BIH = GIF + 512
        GIB = BIH + N3 + 64
        SVB = GIB + 1024
        SMA = SVB + 1024
        SOF = SMA + rows
        assert SOF + 8 * rows <= AR_SZ

        GHB = SOF + 8 * rows
        assert GHB + N3 <= AR_SZ
        gi = FA[0:64, 2048:2048 + D]
        gx = FA[0:64, 3072:3072 + N3]
        ghb = A[0:64, GHB:GHB + N3]
        wk1 = FA[0:64, 0:D]
        wk2 = FA[0:64, 1024:1024 + D]
        wk3 = FA[0:64, 2048:2048 + D]
        gib = A[:, GIB:GIB + D]
        gifm = A[:, GIF:GIF + 512]

        def v8(v):
            k.inc(v.tensor_scalar_add(slotbuf[:, 1024:1025],
                                      slotbuf[:, 1024:1025], 1e-8), "sV")
            k.wait(v, "sV")
            k.inc(v.reciprocal(slotbuf[:, 1026:1027], slotbuf[:, 1024:1025]),
                  "sV")
            k.wait(v, "sV")
            v.tensor_scalar(gi[:], slotbuf[:, 0:D], slotbuf[:, 1026:1027],
                            None, ALU.mult)
            k.inc(v.tensor_copy(gib[0:64, 0:D], gi[:]), "sV")
        self.block(vector=v8)

        def t9(t):
            k.wait(t, "sV")
            for kc in range(8):
                t.transpose(k.pH[:, kc * 64:(kc + 1) * 64],
                            gib[0:64, kc * 128:(kc + 1) * 128],
                            k.ident_b[0:64, 0:64])
            k.inc(t.nop(), "sP")
        self.block(tensor=t9)

        def vA(v):
            k.wait(v, "sP")
            k.inc(v.tensor_copy(gifm[:], k.pH[:, 0:512]), "sV")
        self.block(vector=vA)

        def gB(g):
            k.inc(g.dma_start(ghb, self.gh_dram.ap()), "sG", 16)
            k.inc(g.dma_start(FA[0:1, 0:N3], self.inp["gru_bih"].ap()), "sG", 16)
        self.block(gpsimd=gB)

        def vB(v):
            k.wait(v, "sG")
            k.inc(v.tensor_copy(A[0:1, BIH:BIH + N3], FA[0:1, 0:N3]), "sV")
        self.block(vector=vB)

        BANK6 = [k.pA, k.pB, k.pC, k.pD, k.pE, k.pF]
        p0 = [None]
        for kc in range(8):
            def sC(s, kc=kc):
                if kc >= 2:
                    k.wait(s, "sP", p0[0] + (kc - 1))
                k.inc(s.dma_start(
                    A[:, GRU + (kc % 2) * N3:GRU + (kc % 2) * N3 + N3],
                    bass.AP(self.wbf["wih"], kc * 128, [[D, N3], [1, 128]]),
                    transpose=True), "sL", 16)
            self.block(sync=sC)

            def tC(t, kc=kc):
                k.wait(t, "sL")
                if kc == 0:
                    k.wait(t, "sV")
                for n3c in range(6):
                    i = t.matmul(
                        BANK6[n3c][0:64, 0:512],
                        gifm[:, kc * 64:(kc + 1) * 64],
                        A[:, GRU + (kc % 2) * N3 + n3c * 512:
                          GRU + (kc % 2) * N3 + (n3c + 1) * 512],
                        start=(kc == 0), stop=False)
                if kc == 7:
                    for n3c in range(6):
                        i = t.matmul(
                            BANK6[n3c][0:64, 0:512], k.ones_row[0:1, 0:64],
                            A[0:1, BIH + n3c * 512:BIH + (n3c + 1) * 512],
                            start=False, stop=True)
                k.inc(i, "sP")
            if kc == 0:
                p0[0] = self.cnt("sP")
            self.block(tensor=tC)

        def vD(v):
            k.wait(v, "sP")
            for n3c in range(6):
                i = v.tensor_copy(gx[:, n3c * 512:(n3c + 1) * 512],
                                  BANK6[n3c][0:64, 0:512])
            k.inc(i, "sV")
        self.block(vector=vD)

        def vF(v):
            k.inc(v.tensor_tensor(wk1[:], gx[:, 0:D], ghb[:, 0:D], ALU.add),
                  "sV")
        self.block(vector=vF)

        def aF(a):
            k.wait(a, "sV")
            k.inc(a.activation(wk3[:], wk1[:], AF.Sigmoid), "sA")
        self.block(scalar=aF)

        def vG(v):
            k.wait(v, "sA")
            v.tensor_tensor(wk1[:], wk3[:], ghb[:, 2 * D:3 * D], ALU.mult)
            v.tensor_tensor(wk1[:], wk1[:], gx[:, 2 * D:3 * D], ALU.add)
            k.inc(v.tensor_tensor(wk2[:], gx[:, D:2 * D], ghb[:, D:2 * D],
                                  ALU.add), "sV")
        self.block(vector=vG)

        def aG(a):
            k.wait(a, "sV")
            a.activation(wk3[:], wk1[:], AF.Tanh)
            k.inc(a.activation(gx[:, 0:D], wk2[:], AF.Sigmoid), "sA")
        self.block(scalar=aG)

        def vH(v):
            k.wait(v, "sA")
            v.tensor_tensor(wk1[:], k.hs_slot[:], wk3[:], ALU.subtract)
            v.tensor_tensor(wk1[:], wk1[:], gx[:, 0:D], ALU.mult)
            v.tensor_tensor(gx[:, D:2 * D], wk3[:], wk1[:], ALU.add)
            k.inc(v.tensor_copy(gib[0:64, 0:D], gx[:, D:2 * D]), "sV")
        self.block(vector=vH)

        def tI(t):
            k.wait(t, "sV")
            for kc in range(8):
                t.transpose(k.pH[:, kc * 64:(kc + 1) * 64],
                            gib[0:64, kc * 128:(kc + 1) * 128],
                            k.ident_b[0:64, 0:64])
            k.inc(t.nop(), "sP")
        self.block(tensor=tI)

        def vI(v):
            k.wait(v, "sP")
            k.inc(v.tensor_copy(gifm[:], k.pH[:, 0:512]), "sV")
        self.block(vector=vI)

        WVT = GRU

        def sJ(s):
            k.wait(s, "sP")
            for kc in range(8):
                k.inc(s.dma_start(
                    A[:, WVT + kc * 768:WVT + kc * 768 + 768],
                    bass.AP(self.wbf["wv"], kc * 128, [[D, D], [1, 128]]),
                    transpose=True) if False else s.dma_start(
                    A[:, WVT + kc * D:WVT + (kc + 1) * D] if kc < 6 else
                    A[:, GIB + 1024 + (kc - 6) * D: GIB + 1024 + (kc - 5) * D],
                    bass.AP(self.wbf["wv"], kc * 128, [[D, D], [1, 128]]),
                    transpose=True), "sL", 16)
        WVT2 = SOF

        def sJ2(s):
            for kc in range(8):
                k.inc(s.dma_start(
                    A[:, WVT2 + kc * D:WVT2 + (kc + 1) * D],
                    bass.AP(self.wbf["wv"], kc * 128, [[D, D], [1, 128]]),
                    transpose=True), "sL", 16)
        self.block(sync=sJ2)

        o_bv = self.BR["bv"][0]
        svb = A[:, SVB:SVB + D]

        def tK(t):
            k.wait(t, "sL")
            k.wait(t, "sV")
            for half, bank in ((0, k.pA), (1, k.pB)):
                for kc in range(8):
                    t.matmul(bank[0:64, 0:512],
                             gifm[:, kc * 64:(kc + 1) * 64],
                             A[:, WVT2 + kc * D + half * 512:
                               WVT2 + kc * D + (half + 1) * 512],
                             start=(kc == 0), stop=False)
                k.inc(t.matmul(
                    bank[0:64, 0:512], k.ones_row[0:1, 0:64],
                    k.biasrows[0:1, o_bv + half * 512:o_bv + (half + 1) * 512],
                    start=False, stop=True), "sP")
        self.block(tensor=tK)

        def vK(v):
            k.wait(v, "sP")
            v.tensor_copy(svb[0:64, 0:512], k.pA[0:64, 0:512])
            k.inc(v.tensor_copy(svb[0:64, 512:1024], k.pB[0:64, 0:512]), "sV")
        self.block(vector=vK)
        self.barrier()
        if tag == 1:
            def emit_sv(hnd):
                self.block(gpsimd=lambda g: self.inc(
                    g.dma_start(hnd.ap(), svb[0:64, 0:D]), "sG", 16))
            self.tapout("sv1", emit_sv)

        def sL_(s):
            k.inc(s.dma_start(
                A[:, SMA:SMA + rows],
                bass.AP(self.al_dram, 0, [[128, rows], [1, 128]]),
                transpose=True), "sL", 16)
        self.block(sync=sL_)

        nso = 0
        soa0 = [None]
        for jc in range(8):
            for (c0, cw) in tcols:
                bank = pAB[nso % 2]

                def tso(t, jc=jc, c0=c0, cw=cw, bank=bank, nso=nso):
                    if nso == 0:
                        k.wait(t, "sL")
                    if nso >= 2:
                        k.wait(t, "sA", soa0[0] + (nso - 1))
                    k.inc(t.matmul(bank[:, 0:cw],
                                   svb[0:64, jc * 128:(jc + 1) * 128],
                                   A[0:64, SMA + c0:SMA + c0 + cw],
                                   start=True, stop=True), "sP")

                def aso(a, jc=jc, c0=c0, cw=cw, bank=bank):
                    k.wait(a, "sP")
                    k.inc(a.copy(
                        A[:, SOF + jc * rows + c0:SOF + jc * rows + c0 + cw],
                        bank[:, 0:cw]), "sA")
                if nso == 0:
                    soa0[0] = self.cnt("sA")
                self.block(tensor=tso, scalar=aso)
                nso += 1
        self.barrier()

        WOT = GRU
        WOT = 0
        assert 8 * D <= SMA

        def sM(s):
            for kc in range(8):
                k.inc(s.dma_start(
                    A[:, WOT + kc * D:WOT + (kc + 1) * D],
                    bass.AP(self.wbf["wo"], kc * 128, [[D, D], [1, 128]]),
                    transpose=True), "sL", 16)
        self.block(sync=sM)

        o_bo = self.BR["halfbo"][0]
        v0 = [None]
        for j, t in enumerate(tiles):
            bank0, bank1 = (k.pA, k.pB) if j % 2 == 0 else (k.pC, k.pD)

            def tN(t_, t=t, j=j, bank0=bank0, bank1=bank1):
                if j == 0:
                    k.wait(t_, "sL")
                if j >= 2:
                    k.wait(t_, "sV", v0[0] + (j - 1))
                for half, bank in ((0, bank0), (1, bank1)):
                    for kc in range(8):
                        t_.matmul(bank[:, 0:512],
                                  A[:, SOF + kc * rows + t * 128:
                                    SOF + kc * rows + (t + 1) * 128],
                                  A[:, WOT + kc * D + half * 512:
                                    WOT + kc * D + (half + 1) * 512],
                                  start=(kc == 0), stop=False)
                    k.inc(t_.matmul(
                        bank[:, 0:512], k.ones_row[0:1, :],
                        k.biasrows[0:1, o_bo + half * 512:o_bo + (half + 1) * 512],
                        start=False, stop=True), "sP")
            self.block(tensor=tN)

            def vN(v, t=t, bank0=bank0, bank1=bank1):
                k.wait(v, "sP")
                v.tensor_tensor(k.x_tm[:, t * D:t * D + 512],
                                k.x_tm[:, t * D:t * D + 512],
                                bank0[:, 0:512], ALU.add)
                k.inc(v.tensor_tensor(k.x_tm[:, t * D + 512:(t + 1) * D],
                                      k.x_tm[:, t * D + 512:(t + 1) * D],
                                      bank1[:, 0:512], ALU.add), "sV")
            if j == 0:
                v0[0] = self.cnt("sV")
            self.block(vector=vN)
        self.barrier()


    def attn_proj(self):
        k = self
        A, FA = self.arena, self.farena
        rows = TLOC
        AXN = 0
        AW = AXN + 8 * rows
        AWAB = AW + 8 * D
        AOB = AWAB + 8 * 2 * NH
        assert AOB + 2 * D <= AR_SZ

        self.xnfm_load(self.xn_dram, AXN)
        self.barrier()

        sq = FA[:, 0:D]
        ss = FA[:, 1024:1040]
        sn = FA[:, 1040:1056]
        rn = FA[:, 1056:1072]

        for wnm, dest, tl, epi in [
            ("wq", self.q_dram, list(range(HT, NT)), "l2"),
            ("wk", self.k_dram, list(range(NT)), "l2"),
            ("wvp", self.v_dram, list(range(NT)), "copy"),
            ("wg", self.gate_dram, list(range(HT, NT)), "silu"),
        ]:
            def sW(s, wnm=wnm):
                for kc in range(8):
                    k.inc(s.dma_start(
                        A[:, AW + kc * D:AW + (kc + 1) * D],
                        bass.AP(self.wbf[wnm], kc * 128 * D, [[D, 128], [1, D]])),
                        "sL", 16)
            self.block(sync=sW)
            vprev = [None]
            for j, t in enumerate(tl):
                bank0, bank1 = (k.pA, k.pB) if j % 2 == 0 else (k.pC, k.pD)
                obuf = A[:, AOB + (j % 2) * D:AOB + (j % 2) * D + D]

                def tP(t_, t=t, j=j, bank0=bank0, bank1=bank1):
                    if j == 0:
                        k.wait(t_, "sL")
                    if j >= 2:
                        k.wait(t_, "sV", vprev[0])
                    for half, bank in ((0, bank0), (1, bank1)):
                        for kc in range(8):
                            i = t_.matmul(
                                bank[:, 0:512],
                                A[:, AXN + kc * rows + t * 128:
                                  AXN + kc * rows + (t + 1) * 128],
                                A[:, AW + kc * D + half * 512:
                                  AW + kc * D + (half + 1) * 512],
                                start=(kc == 0), stop=(kc == 7))
                    k.inc(i, "sP")
                self.block(tensor=tP)

                if epi == "l2":
                    def aP(a, bank0=bank0, bank1=bank1):
                        k.wait(a, "sP")
                        k.wait(a, "sV")
                        a.activation(sq[:, 0:512], bank0[:, 0:512], AF.Square)
                        k.inc(a.activation(sq[:, 512:1024], bank1[:, 0:512],
                                           AF.Square), "sA")
                    self.block(scalar=aP)

                    def vP(v, bank0=bank0, bank1=bank1, obuf=obuf):
                        k.wait(v, "sA")
                        k.inc(v.reduce_sum(
                            bass.AP(FA, 1024, [[FA_SZ, 128], [1, 16], [1, 1]]),
                            sq.rearrange("p (h e) -> p h e", h=16), axis=AX.X),
                            "sV")
                    self.block(vector=vP)

                    def aP2(a):
                        k.wait(a, "sV")
                        k.inc(a.activation(sn[:], ss[:], AF.Sqrt), "sA")
                    self.block(scalar=aP2)

                    def vP2(v, bank0=bank0, bank1=bank1, obuf=obuf):
                        k.wait(v, "sA")
                        k.inc(v.tensor_scalar_max(rn, sn, 1e-12), "sV")
                        k.wait(v, "sV")
                        k.inc(v.reciprocal(sn, rn), "sV")
                        k.wait(v, "sV")
                        rep0 = bass.AP(FA, 1040, [[FA_SZ, 128], [1, 8], [0, 64]])
                        rep1 = bass.AP(FA, 1048, [[FA_SZ, 128], [1, 8], [0, 64]])
                        v.tensor_tensor(
                            obuf[:, 0:512].rearrange("p (h e) -> p h e", h=8),
                            bass.AP(bank0, 0, [[512, 128], [64, 8], [1, 64]]),
                            rep0, ALU.mult)
                        k.inc(v.tensor_tensor(
                            obuf[:, 512:1024].rearrange("p (h e) -> p h e", h=8),
                            bass.AP(bank1, 0, [[512, 128], [64, 8], [1, 64]]),
                            rep1, ALU.mult), "sV")
                    self.block(vector=vP2)
                elif epi == "copy":
                    def vP3(v, bank0=bank0, bank1=bank1, obuf=obuf):
                        k.wait(v, "sP")
                        v.tensor_copy(obuf[:, 0:512], bank0[:, 0:512])
                        k.inc(v.tensor_copy(obuf[:, 512:1024], bank1[:, 0:512]),
                              "sV")
                    self.block(vector=vP3)
                else:
                    def aP4(a, bank0=bank0, bank1=bank1, obuf=obuf):
                        k.wait(a, "sP")
                        a.activation(obuf[:, 0:512], bank0[:, 0:512], AF.Silu)
                        k.inc(a.activation(obuf[:, 512:1024], bank1[:, 0:512],
                                           AF.Silu), "sA")
                    self.block(scalar=aP4)

                    def vP4(v):
                        k.wait(v, "sA")
                        k.inc(v.engine_nop(), "sV")
                    self.block(vector=vP4)
                vprev[0] = self.cnt("sV")

                def gP(g, t=t, obuf=obuf, dest=dest):
                    k.wait(g, "sV")
                    k.inc(g.dma_start(
                        bass.AP(dest, t * 128 * D, [[D, 128], [1, D]]),
                        obuf), "sG", 16)
                self.block(gpsimd=gP)
            self.barrier()

        def sAB(s):
            for kc in range(8):
                k.inc(s.dma_start(
                    A[:, AWAB + kc * 32:AWAB + (kc + 1) * 32],
                    bass.AP(self.wbf["wab"], kc * 128 * 32, [[32, 128], [1, 32]])),
                    "sL", 16)
        self.block(sync=sAB)
        o_ba = self.BR["ba"][0]
        zt = FA[:, 0:32]
        t1 = FA[:, 32:64]
        t1b = FA[:, 96:128]
        t2 = FA[:, 64:96]
        for j, t in enumerate(range(NT)):
            def tAB(t_, t=t, j=j):
                if j == 0:
                    k.wait(t_, "sL")
                if j >= 1:
                    k.wait(t_, "sV", self._ab_v[j - 1])
                for kc in range(8):
                    t_.matmul(k.pE[:, 0:32],
                              A[:, AXN + kc * rows + t * 128:
                                AXN + kc * rows + (t + 1) * 128],
                              A[:, AWAB + kc * 32:AWAB + (kc + 1) * 32],
                              start=(kc == 0), stop=False)
                k.inc(t_.matmul(k.pE[:, 0:32], k.ones_row[0:1, :],
                                k.biasrows[0:1, o_ba:o_ba + 32],
                                start=False, stop=True), "sP")
            self.block(tensor=tAB)

            def vAB(v, t=t):
                k.wait(v, "sP")
                k.wait(v, "sA")
                k.inc(v.tensor_copy(zt[:], k.pE[:, 0:32]), "sV")
            self.block(vector=vAB)

            def aAB(a, t=t):
                k.wait(a, "sV")
                a.activation(k.ab_tm[:, t * 48 + 32:t * 48 + 48],
                             zt[:, 16:32], AF.Sigmoid)
                a.activation(t1[:], zt[:], AF.Abs)
                a.activation(t1b[:], t1[:], AF.Exp, scale=-1.0)
                k.inc(a.activation(t1[:], t1b[:], AF.Ln, bias=1.0), "sA")
            self.block(scalar=aAB)

            def vAB2(v, t=t, j=j):
                k.wait(v, "sA")
                k.inc(v.tensor_scalar_min(t2[:], zt[:], 0.0), "sV")
                k.wait(v, "sV")
                k.inc(v.tensor_tensor(k.ab_tm[:, t * 48:t * 48 + 32],
                                      t2[:], t1[:], ALU.subtract), "sV")
            if j == 0:
                self._ab_v = []
            self.block(vector=vAB2)
            self._ab_v.append(self.cnt("sV"))
            if j == 0 and "dbgab" in self.taps and "dbgab" not in self.tap_handles:
                def emit_dbgab2(hnd):
                    self.block(gpsimd=lambda g: self.inc(g.dma_start(
                        bass.AP(hnd, 0, [[320, 128], [1, 320]]),
                        FA[:, 0:320]), "sG", 16))
                self.tapout("dbgab", emit_dbgab2)
            if getattr(self, "_paranoid_ab", False):
                self.barrier()
        self.barrier()

        def vHG(v):
            for t in range(HT):
                i = v.tensor_scalar(k.ab_tm[:, t * 48 + 32:t * 48 + 48],
                                    k.ab_tm[:, t * 48 + 32:t * 48 + 48],
                                    k.gate_col[:, 0:1], None, ALU.mult)
            k.inc(i, "sV")
        self.block(vector=vHG)
        self.barrier()

    def attn_lg(self):
        k = self
        FA = self.farena
        la_f = FA[0:16, 2048:2048 + 128]
        lg_f = FA[0:16, 2176:2176 + 128]
        lgc = self.fmrow[0:16, 0:NT]
        for c in range(NT):
            def tL(t_, c=c):
                if c >= 1:
                    k.wait(t_, "sV", self._lg_v[c - 1])
                k.inc(t_.transpose(k.pE[0:16, 0:128],
                                   k.ab_tm[:, c * 48:c * 48 + 16],
                                   k.ident_f[:, 0:128]), "sP")
            self.block(tensor=tL)

            def vL(v, c=c):
                k.wait(v, "sP")
                k.inc(v.tensor_copy(la_f[:], k.pE[0:16, 0:128]), "sV")
                k.wait(v, "sV")
                k.inc(v.tensor_tensor_scan(lg_f[:], la_f[:], la_f[:], 0.0,
                                           ALU.add, ALU.bypass), "sV")
                k.wait(v, "sV")
                k.inc(v.tensor_copy(lgc[:, c:c + 1], lg_f[:, 127:128]), "sV")
            self.block(vector=vL)

            def tL2(t_, c=c):
                k.wait(t_, "sV")
                k.inc(t_.transpose(k.pF[:, 0:16], lg_f[:],
                                   k.ident_f[0:16, 0:16]), "sP")
            self.block(tensor=tL2)

            def vL2(v, c=c):
                k.wait(v, "sP")
                k.inc(v.tensor_copy(k.lg_tm[:, c * 16:(c + 1) * 16],
                                    k.pF[:, 0:16]), "sV")
            if c == 0:
                self._lg_v = []
            self.block(vector=vL2)
            self._lg_v.append(self.cnt("sV"))

        def tL3(t_):
            k.wait(t_, "sV")
            k.inc(t_.transpose(k.pE[0:NT, 0:16], lgc[:, 0:NT],
                               k.ident_f[0:16, 0:16]), "sP")
        self.block(tensor=tL3)

        def vL3(v):
            k.wait(v, "sP")
            k.inc(v.tensor_copy(FA[0:NT, 2048:2064], k.pE[0:NT, 0:16]), "sV")
        self.block(vector=vL3)

        def gL(g):
            k.wait(g, "sV")
            k.inc(g.dma_start(self.lgc_dram.ap(), FA[0:NT, 2048:2064]), "sG", 16)
            k.wait(g, "sG")
            k.inc(g.dma_start(k.lgc_flat[0:1, 0:NT * NH],
                              bass.AP(self.lgc_dram, 0, [[NT * NH, 1], [1, NT * NH]])),
                  "sG", 16)
        self.block(gpsimd=gL)
        self.barrier()

    def attn_scan(self):
        k = self
        A, FA = self.arena, self.farena
        rows = TLOC
        SC_KF = 0
        SC_QF = SC_KF + 2 * 1024
        SC_KT = SC_QF + 2 * 1024
        SC_VT = SC_KT + 2 * 1024
        SC_PT = SC_VT + 2 * 1024
        SC_NT_ = SC_PT + 16 * 128
        SC_NN = SC_NT_ + 16 * 128
        SC_P2 = SC_NN + 16 * 128
        SC_P2T = SC_P2 + 16 * 128
        SC_P4T = SC_P2T + 16 * 128
        SC_R = SC_P4T + 16 * 128
        SC_U1 = SC_R + 1024
        SC_U2 = SC_U1 + 1024
        SC_U3 = SC_U2 + 1024
        SC_KPP = SC_U3 + 1024
        SC_GT = SC_KPP + 1024
        SC_YF = SC_GT + 2 * 1024
        assert SC_YF + 1024 <= AR_SZ
        FY = 0
        FD = 1024
        FRT = 1280
        FCOL = 1536
        FBV = 2048
        FSQ = 3072
        FST = 4160
        FGN = 4224
        assert FGN + 1024 <= FA_SZ

        o_gn = self.BR["gnorm"][0]

        def tG(t_):
            for half in range(2):
                k.inc(t_.matmul(k.pE[:, 0:512] if half == 0 else k.pF[:, 0:512],
                                k.ones_row[0:1, :],
                                k.biasrows[0:1, o_gn + half * 512:
                                           o_gn + (half + 1) * 512],
                                start=True, stop=True), "sP")
        self.block(tensor=tG)

        def vG_(v):
            k.wait(v, "sP")
            v.tensor_copy(FA[:, FGN:FGN + 512], k.pE[:, 0:512])
            k.inc(v.tensor_copy(FA[:, FGN + 512:FGN + 1024], k.pF[:, 0:512]),
                  "sV")
        self.block(vector=vG_)
        self.barrier()

        chend = []
        for c in range(NT):
            real = c >= HT
            par = c % 2
            kf = A[:, SC_KF + par * 1024:SC_KF + par * 1024 + 1024]
            qf = A[:, SC_QF + par * 1024:SC_QF + par * 1024 + 1024]
            kt = A[:, SC_KT + par * 1024:SC_KT + par * 1024 + 1024]
            vt = A[:, SC_VT + par * 1024:SC_VT + par * 1024 + 1024]
            gt = A[:, SC_GT + par * 1024:SC_GT + par * 1024 + 1024]

            def sS(s, c=c, real=real, kf=kf, qf=qf, kt=kt, vt=vt, gt=gt):
                if c >= 2:
                    k.wait(s, "sP", chend[c - 2])
                for kc in range(8):
                    k.inc(s.dma_start(
                        kf[:, kc * 128:(kc + 1) * 128],
                        bass.AP(self.k_dram, c * 128 * D + kc * 128,
                                [[D, 128], [1, 128]]), transpose=True), "sL", 16)
                k.inc(s.dma_start(
                    kt, bass.AP(self.k_dram, c * 128 * D, [[D, 128], [1, D]])),
                    "sL", 16)
                k.inc(s.dma_start(
                    vt, bass.AP(self.v_dram, c * 128 * D, [[D, 128], [1, D]])),
                    "sL", 16)
                if real:
                    for kc in range(8):
                        k.inc(s.dma_start(
                            qf[:, kc * 128:(kc + 1) * 128],
                            bass.AP(self.q_dram, c * 128 * D + kc * 128,
                                    [[D, 128], [1, 128]]), transpose=True),
                            "sL", 16)
                    k.inc(s.dma_start(
                        gt, bass.AP(self.gate_dram, c * 128 * D, [[D, 128], [1, D]])),
                        "sL", 16)
            self.block(sync=sS)
            sl_done = self.cnt("sL")

            lgc_ = k.lg_tm[:, c * 16:(c + 1) * 16]
            lac_ = k.ab_tm[:, c * 48:c * 48 + 16]
            lbc_ = k.ab_tm[:, c * 48 + 16:c * 48 + 32]
            bsc_ = k.ab_tm[:, c * 48 + 32:c * 48 + 48]
            shb = FA[:, FCOL:FCOL + 16]
            negshbE = FA[:, FCOL + 16:FCOL + 32]
            explg = FA[:, FCOL + 32:FCOL + 48]
            neglg = FA[:, FCOL + 48:FCOL + 64]
            kscale = FA[:, FCOL + 64:FCOL + 80]
            explgC = FA[:, FCOL + 80:FCOL + 96]

            def tC0(t_, c=c):
                k.wait(t_, "sV")
                t_.matmul(k.pG[:, 448:464], k.onesf_row[0:1, :],
                          k.lgc_flat[0:1, c * 16:(c + 1) * 16],
                          start=True, stop=True)
                k.inc(t_.nop(), "sP")
            self.block(tensor=tC0)

            def vC0(v, c=c):
                k.wait(v, "sP")
                k.inc(v.tensor_tensor(shb[:], lgc_, lac_, ALU.subtract), "sV")
                k.wait(v, "sV")
                k.inc(v.tensor_tensor(shb[:], shb[:], lbc_, ALU.add), "sV")
                v.tensor_scalar_mul(neglg[:], lgc_, -1.0)
                k.inc(v.tensor_tensor(FA[:, FCOL + 96:FCOL + 112],
                                      k.pG[:, 448:464], lgc_,
                                      ALU.subtract), "sV")
            self.block(vector=vC0)

            def aC0(a, c=c):
                k.wait(a, "sV")
                a.activation(explg[:], lgc_, AF.Exp)
                a.activation(kscale[:], FA[:, FCOL + 96:FCOL + 112], AF.Exp)
                a.activation(explgC[:], k.pG[:, 448:464], AF.Exp)
                k.inc(a.activation(FA[:, FCOL + 112:FCOL + 128], shb[:],
                                   AF.Exp), "sA")
            self.block(scalar=aC0)

            def vC1(v, c=c):
                k.wait(v, "sA")
                if c < HT:
                    i = v.tensor_scalar(negshbE[:], FA[:, FCOL + 112:FCOL + 128],
                                        k.gateneg_col[:, 0:1], None, ALU.mult)
                else:
                    i = v.tensor_scalar_mul(negshbE[:],
                                            FA[:, FCOL + 112:FCOL + 128], -1.0)
                k.inc(i, "sV")
            self.block(vector=vC1)

            def gC(g, c=c, kt=kt, vt=vt):
                k.wait(g, "sL", sl_done)
                k.wait(g, "sV")
                bsr = bass.AP(k.ab_tm, c * 48 + 32, [[NT * 48, 128], [1, 16], [0, 64]])
                g.tensor_tensor(FA[:, FBV:FBV + 1024].rearrange(
                    "p (h e) -> p h e", h=16), vt.rearrange(
                    "p (h e) -> p h e", h=16), bsr, ALU.mult)
                ksr = bass.AP(FA, FCOL + 64, [[FA_SZ, 128], [1, 16], [0, 64]])
                k.inc(g.tensor_tensor(A[:, SC_KPP:SC_KPP + 1024].rearrange(
                    "p (h e) -> p h e", h=16), kt.rearrange(
                    "p (h e) -> p h e", h=16), ksr, ALU.mult), "sO")
            self.block(gpsimd=gC)

            for h in range(NH):
                hp = h % 2
                hq = h // 2
                base = hp * 64
                kfs = kf[base:base + 128 - 64, hq * 128:(hq + 1) * 128]                     if False else bass.AP(kf, hq * 128, [[1024, 64], [1, 128]],
                                          ) if False else None
                kfs = kf[base:base + 64, hq * 128:(hq + 1) * 128]
                qfs = qf[base:base + 64, hq * 128:(hq + 1) * 128]
                Ms = k.M_bf[base:base + 64, hq * 64:(hq + 1) * 64]
                Mfs = k.M_all[base:base + 64, hq * 64:(hq + 1) * 64]
                pS = [k.pA, k.pB][hp]
                pD_ = [k.pC, k.pD][hp]
                pSo = [k.pE, k.pF][hp]
                Pt = A[:, SC_PT + h * 128:SC_PT + (h + 1) * 128]
                Nt = A[:, SC_NT_ + h * 128:SC_NT_ + (h + 1) * 128]
                Nn = A[:, SC_NN + h * 128:SC_NN + (h + 1) * 128]
                P2 = A[:, SC_P2 + h * 128:SC_P2 + (h + 1) * 128]
                P2t = A[:, SC_P2T + h * 128:SC_P2T + (h + 1) * 128]
                P4t = A[:, SC_P4T + h * 128:SC_P4T + (h + 1) * 128]
                Rs = A[:, SC_R + h * 64:SC_R + (h + 1) * 64]
                U1 = A[:, SC_U1 + h * 64:SC_U1 + (h + 1) * 64]
                U2 = A[:, SC_U2 + h * 64:SC_U2 + (h + 1) * 64]
                U3 = A[:, SC_U3 + h * 64:SC_U3 + (h + 1) * 64]
                Kpp = A[:, SC_KPP + h * 64:SC_KPP + (h + 1) * 64]
                rt1 = FA[:, FRT:FRT + 128]
                rt2 = FA[:, FRT + 128:FRT + 256]
                eD1 = FA[:, FD:FD + 128]
                eD2 = FA[:, FD + 128:FD + 256]

                def vR(v, h=h, real=real):
                    if h == 0:
                        k.wait(v, "sO")
                    k.wait(v, "sA")
                    i = v.tensor_scalar(rt2[:], k.mask_strict[:],
                                        shb[:, h:h + 1], None, ALU.add)
                    if real:
                        i = v.tensor_scalar(rt1[:], k.mask_incl[:],
                                            lgc_[:, h:h + 1], None, ALU.add)
                    k.inc(i, "sV")
                self.block(vector=vR)

                def tS(t_, h=h, real=real, kfs=kfs, qfs=qfs):
                    if h == 0:
                        k.wait(t_, "sL", sl_done)
                    k.wait(t_, "sV")
                    if h >= 2:
                        k.wait(t_, "sV", self._sc_sv[h - 2])
                    t_.matmul(pS[:, 0:128], kfs, kfs, start=True, stop=True)
                    if real:
                        t_.matmul(pS[:, 128:256], kfs, qfs, start=True, stop=True)
                    t_.transpose(pD_[:, 128:256], rt2[:], k.ident_f[:])
                    if real:
                        t_.transpose(pD_[:, 0:128], rt1[:], k.ident_f[:])
                    k.inc(t_.nop(), "sP")
                self.block(tensor=tS)

                def aD(a, h=h, real=real):
                    k.wait(a, "sP")
                    i = a.activation(eD2[:], pD_[:, 128:256], AF.Exp,
                                     bias=neglg[:, h:h + 1])
                    if real:
                        i = a.activation(eD1[:], pD_[:, 0:128], AF.Exp,
                                         bias=neglg[:, h:h + 1])
                    k.inc(i, "sA")
                self.block(scalar=aD)

                def vD_(v, h=h, real=real):
                    k.wait(v, "sA")
                    i = v.scalar_tensor_tensor(Nt, pS[:, 0:128], -1.0, eD2[:],
                                               ALU.mult, ALU.mult)
                    if real:
                        i = v.scalar_tensor_tensor(Pt, pS[:, 128:256], 1.0,
                                                   eD1[:], ALU.mult, ALU.mult)
                    k.inc(i, "sV")
                self.block(vector=vD_)
                if h == 0:
                    self._sc_sv = []
                self._sc_sv.append(self.cnt("sV"))

                def tN(t_, h=h):
                    k.wait(t_, "sV")
                    t_.transpose(k.pH[:, 0:128], Nt, k.ident_b[:])
                    k.inc(t_.nop(), "sP")
                self.block(tensor=tN)

                def vN_(v, h=h):
                    k.wait(v, "sP")
                    k.inc(v.tensor_copy(Nn, k.pH[:, 0:128]), "sV")
                self.block(vector=vN_)

                def tQ(t_, h=h):
                    k.wait(t_, "sV")
                    t_.matmul(pSo[:, 0:128], Nt, Nn, start=True, stop=True)
                    t_.matmul(pSo[:, 128:256], Nn, Nt, start=True, stop=True)
                    k.inc(t_.nop(), "sP")
                self.block(tensor=tQ)

                def vQ(v, h=h):
                    k.wait(v, "sP")
                    v.tensor_copy(P2, pSo[:, 0:128])
                    k.inc(v.tensor_copy(P2t, pSo[:, 128:256]), "sV")
                self.block(vector=vQ)

                def tQ2(t_, h=h):
                    k.wait(t_, "sV")
                    t_.matmul(pSo[:, 256:384], P2, P2t, start=True, stop=True)
                    k.inc(t_.nop(), "sP")
                self.block(tensor=tQ2)

                def vQ2(v, h=h):
                    k.wait(v, "sP")
                    k.inc(v.tensor_copy(P4t, pSo[:, 256:384]), "sV")
                self.block(vector=vQ2)

                def tM(t_, h=h, real=real, kfs=kfs, qfs=qfs, Ms=Ms):
                    k.wait(t_, "sV")
                    t_.matmul(k.pG[:, 0:64], kfs, Ms, start=True, stop=True)
                    if real:
                        t_.matmul(k.pG[:, 64:128], qfs, Ms, start=True,
                                  stop=True)
                    k.inc(t_.nop(), "sP")
                self.block(tensor=tM)

                def vRB(v, h=h):
                    k.wait(v, "sP")
                    k.inc(v.scalar_tensor_tensor(
                        Rs, k.pG[:, 0:64], negshbE[:, h:h + 1],
                        FA[:, FBV + h * 64:FBV + (h + 1) * 64],
                        ALU.mult, ALU.add), "sV")
                self.block(vector=vRB)

                def tA1(t_, h=h):
                    k.wait(t_, "sV")
                    t_.matmul(k.pG[:, 128:192], Nt, Rs, start=True, stop=True)
                    k.inc(t_.nop(), "sP")
                self.block(tensor=tA1)

                def vA1(v, h=h):
                    k.wait(v, "sP")
                    k.inc(v.tensor_tensor(U1, k.pG[:, 128:192], Rs, ALU.add),
                          "sV")
                self.block(vector=vA1)

                def tA2(t_, h=h):
                    k.wait(t_, "sV")
                    t_.matmul(k.pG[:, 192:256], P2t, U1, start=True, stop=True)
                    k.inc(t_.nop(), "sP")
                self.block(tensor=tA2)

                def vA2(v, h=h):
                    k.wait(v, "sP")
                    k.inc(v.tensor_tensor(U2, k.pG[:, 192:256], U1, ALU.add),
                          "sV")
                self.block(vector=vA2)

                def tA3(t_, h=h):
                    k.wait(t_, "sV")
                    t_.matmul(k.pG[:, 256:320], P4t, U2, start=True, stop=True)
                    k.inc(t_.nop(), "sP")
                self.block(tensor=tA3)

                def vA3(v, h=h):
                    k.wait(v, "sP")
                    k.inc(v.tensor_tensor(U3, k.pG[:, 256:320], U2, ALU.add),
                          "sV")
                self.block(vector=vA3)

                if real:
                    def tO(t_, h=h, qfs=qfs, Ms=Ms):
                        k.wait(t_, "sV")
                        t_.matmul(k.pG[:, 320:384], Pt, U3, start=True,
                                  stop=True)
                        t_.matmul(k.pG[:, 384:448], qfs, Ms, start=True,
                                  stop=True)
                        k.inc(t_.nop(), "sP")
                    self.block(tensor=tO)

                    def vO(v, h=h):
                        k.wait(v, "sP")
                        k.inc(v.tensor_scalar(FA[:, FD:FD + 64],
                                              k.pG[:, 384:448],
                                              explg[:, h:h + 1], None,
                                              ALU.mult), "sV")
                        k.wait(v, "sV")
                        k.inc(v.tensor_tensor(
                            FA[:, FY + h * 64:FY + (h + 1) * 64],
                            FA[:, FD:FD + 64], k.pG[:, 320:384], ALU.add),
                            "sV")
                    self.block(vector=vO)

                def tU(t_, h=h, hp=hp, Kpp=Kpp):
                    k.wait(t_, "sV")
                    t_.matmul(k.pG[base:base + 64, 448:512], Kpp, U3,
                              start=True, stop=True, tile_position=(0, base))
                    k.inc(t_.nop(), "sP")
                self.block(tensor=tU)

                def vU(v, h=h, base=base, Mfs=Mfs, Ms=Ms):
                    k.wait(v, "sP")
                    k.inc(v.scalar_tensor_tensor(
                        Mfs, Mfs, explgC[base:base + 64, h:h + 1],
                        k.pG[base:base + 64, 448:512], ALU.mult, ALU.add), "sV")
                    k.wait(v, "sV")
                    k.inc(v.tensor_copy(Ms, Mfs), "sV")
                self.block(vector=vU)
            chend.append(self.cnt("sP"))

            if real:
                ssy = FA[:, FST:FST + 1]
                rsy = FA[:, FST + 1:FST + 2]

                def aY(a, c=c):
                    k.wait(a, "sV")
                    k.inc(a.activation(FA[:, FSQ:FSQ + 1024],
                                       FA[:, FY:FY + 1024], AF.Square,
                                       accum_out=ssy[:, 0:1]), "sA")
                self.block(scalar=aY)

                def vY(v, c=c):
                    k.wait(v, "sA")
                    k.inc(v.tensor_scalar_mul(rsy[:], ssy[:], 1.0 / D), "sV")
                self.block(vector=vY)

                def aY2(a, c=c):
                    k.wait(a, "sV")
                    a.activation(FA[:, FST + 2:FST + 3], rsy[:], AF.Ln,
                                 bias=LN_EPS)
                    k.inc(a.activation(rsy[:], FA[:, FST + 2:FST + 3], AF.Exp,
                                       scale=-0.5), "sA")
                self.block(scalar=aY2)

                def vY2(v, c=c, gt=gt, par=par):
                    k.wait(v, "sA")
                    k.wait(v, "sG")
                    k.wait(v, "sL", sl_done)
                    v.tensor_tensor(FA[:, FSQ:FSQ + 1024], gt,
                                    FA[:, FGN:FGN + 1024], ALU.mult)
                    k.inc(v.scalar_tensor_tensor(
                        A[:, SC_YF:SC_YF + 1024], FA[:, FY:FY + 1024],
                        rsy[:, 0:1], FA[:, FSQ:FSQ + 1024],
                        ALU.mult, ALU.mult), "sV")
                self.block(vector=vY2)

                def gY(g, c=c):
                    k.wait(g, "sV")
                    k.inc(g.dma_start(
                        bass.AP(self.yf_dram, c * 128 * D, [[D, 128], [1, D]]),
                        A[:, SC_YF:SC_YF + 1024]), "sG", 16)
                self.block(gpsimd=gY)
        self.barrier()

    def attn_wo(self):
        k = self
        A = self.arena
        rows = TLOC
        YF = 0
        WO = YF + 8 * rows
        assert WO + 8 * D <= AR_SZ

        def s1(s):
            for kc in range(8):
                k.inc(s.dma_start(
                    A[:, YF + kc * rows:YF + (kc + 1) * rows],
                    bass.AP(self.yf_dram, kc * 128, [[D, rows], [1, 128]]),
                    transpose=True), "sL", 16)
            for kc in range(8):
                k.inc(s.dma_start(
                    A[:, WO + kc * D:WO + (kc + 1) * D],
                    bass.AP(self.wbf["woa"], kc * 128 * D, [[D, 128], [1, D]])),
                    "sL", 16)
        self.block(sync=s1)

        v0 = [None]
        for j, t in enumerate(range(HT, NT)):
            bank0, bank1 = (k.pA, k.pB) if j % 2 == 0 else (k.pC, k.pD)

            def tN(t_, t=t, j=j, bank0=bank0, bank1=bank1):
                if j == 0:
                    k.wait(t_, "sL")
                if j >= 2:
                    k.wait(t_, "sV", v0[0] + (j - 1))
                for half, bank in ((0, bank0), (1, bank1)):
                    for kc in range(8):
                        i = t_.matmul(
                            bank[:, 0:512],
                            A[:, YF + kc * rows + t * 128:
                              YF + kc * rows + (t + 1) * 128],
                            A[:, WO + kc * D + half * 512:
                              WO + kc * D + (half + 1) * 512],
                            start=(kc == 0), stop=(kc == 7))
                k.inc(i, "sP")
            self.block(tensor=tN)

            def vN(v, t=t, bank0=bank0, bank1=bank1):
                k.wait(v, "sP")
                v.tensor_tensor(k.x_tm[:, t * D:t * D + 512],
                                k.x_tm[:, t * D:t * D + 512],
                                bank0[:, 0:512], ALU.add)
                k.inc(v.tensor_tensor(k.x_tm[:, t * D + 512:(t + 1) * D],
                                      k.x_tm[:, t * D + 512:(t + 1) * D],
                                      bank1[:, 0:512], ALU.add), "sV")
            if j == 0:
                v0[0] = self.cnt("sV")
            self.block(vector=vN)
        self.barrier()

    def finish(self):
        k = self

        def g(g_):
            for t in range(HT, NT):
                k.inc(g_.dma_start(
                    bass.AP(self.out, (t - HT) * 128 * D, [[D, 128], [1, D]]),
                    k.x_tm[:, t * D:(t + 1) * D]), "sG", 16)
            k.wait(g_, "sG")
        self.block(gpsimd=g)

    def build(self):
        return self.nc, self.tap_handles


def _build(taps=()):
    kb = KB(taps)
    kb.boot()
    kb.gh_precompute()
    kb.ln(list(range(NT)))
    kb.tap_dram("xn1", kb.xn_dram)
    kb.ffn(list(range(NT)), tag=1)
    kb.tap_xtm("x1", list(range(NT)))
    kb.ln(list(range(NT)))
    kb.attn_proj()
    kb.tap_dram("qn", kb.q_dram)
    kb.tap_dram("kn", kb.k_dram)
    kb.tap_dram("vv", kb.v_dram)
    if "abtm" in kb.taps:
        kb.tapout("abtm", lambda hnd: kb.block(gpsimd=lambda g: kb.inc(
            g.dma_start(bass.AP(hnd, 0, [[NT * 48, 128], [1, NT * 48]]),
                        kb.ab_tm[:]), "sG", 16)))
    kb.attn_lg()
    if "lgtm" in kb.taps:
        kb.tapout("lgtm", lambda hnd: kb.block(gpsimd=lambda g: kb.inc(
            g.dma_start(bass.AP(hnd, 0, [[NT * NH, 128], [1, NT * NH]]),
                        kb.lg_tm[:]), "sG", 16)))
    kb.attn_scan()
    kb.tap_dram("yf", kb.yf_dram)
    kb.attn_wo()
    kb.tap_xtm("x2", list(range(NT)))
    kb.ln(list(range(HT, NT)))
    kb.ffn(list(range(HT, NT)), tag=2)
    kb.finish()
    return kb.build()


_CACHE = {}


def _make_inmaps(inputs):
    x = np.asarray(inputs["L0_attn_x"], np.float32)
    w = {
        "Wq": inputs["L0_attn_Wq"], "Wk": inputs["L0_attn_Wk"],
        "Wv": inputs["L0_attn_Wv"], "Wa": inputs["L0_attn_Wa"],
        "Wb": inputs["L0_attn_Wb"], "Wg": inputs["L0_attn_Wg"],
        "Wo": inputs["L0_attn_Wo"],
        "ba": np.reshape(inputs["L0_attn_ba"], (1, -1)),
        "bb": np.reshape(inputs["L0_attn_bb"], (1, -1)),
        "g_norm": np.reshape(inputs["L0_attn_g_norm"], (1, -1)),
        "slot_init": inputs["L0_ffn_slot_init"],
        "slot_scale": inputs["L0_ffn_slot_scale"],
        "w1": inputs["L0_ffn_w1"],
        "b1": np.reshape(inputs["L0_ffn_b1"], (1, -1)),
        "w2": inputs["L0_ffn_w2"],
        "b2": np.reshape(inputs["L0_ffn_b2"], (1, -1)),
        "gru_wih": inputs["L0_ffn_gru_wih"],
        "gru_whh": inputs["L0_ffn_gru_whh"],
        "gru_bih": np.reshape(inputs["L0_ffn_gru_bih"], (1, -1)),
        "gru_bhh": np.reshape(inputs["L0_ffn_gru_bhh"], (1, -1)),
        "wv": inputs["L0_ffn_wv"],
        "bv": np.reshape(inputs["L0_ffn_bv"], (1, -1)),
        "wo": inputs["L0_ffn_wo"],
        "bo": np.reshape(inputs["L0_ffn_bo"], (1, -1)),
        "tau": np.reshape(inputs["L0_ffn_tau"], (1, 1)),
    }
    w = {nm: np.ascontiguousarray(np.asarray(a, np.float32)) for nm, a in w.items()}
    in_maps = []
    for c in range(8):
        b, s = c // 4, c % 4
        t0 = s * 2048
        if s == 0:
            xs = np.concatenate([x[b, 0:384], x[b, 0:2048]], axis=0)
            hg = 0.0
        else:
            xs = x[b, t0 - 384:t0 + 2048]
            hg = 1.0
        m = dict(w)
        m["x_slice"] = np.ascontiguousarray(xs)
        m["halo_gate"] = np.full((128, 1), hg, np.float32)
        in_maps.append(m)
    return in_maps


def kernel(**inputs):
    if "prog" not in _CACHE:
        _CACHE["prog"] = _build()
    nc, _ = _CACHE["prog"]
    in_maps = _make_inmaps(inputs)
    res = run_bass_kernel_spmd(nc, in_maps, core_ids=list(range(8)))
    out = np.empty((2, 8192, 1024), np.float32)
    for c in range(8):
        b, s = c // 4, c % 4
        out[b, s * 2048:(s + 1) * 2048] = res.results[c]["out"]
    return out


# version 15
# speedup vs baseline: 1.3374x; 1.3374x over previous
import numpy as np
import ml_dtypes
import concourse.bass as bass
import concourse.mybir as mybir
from concourse.bass_utils import run_bass_kernel_spmd

F32 = mybir.dt.float32
BF16 = mybir.dt.bfloat16
AF = mybir.ActivationFunctionType
ALU = mybir.AluOpType
AX = mybir.AxisListType

D = 1024
NH, HD = 16, 64
HT, RT = 3, 16
NT = HT + RT
TLOC, TREAL = NT * 128, RT * 128
NSLOT, TOPK, HFF = 64, 8, 512
N3 = 3 * D
LN_EPS = 1e-5
AR_SZ = 38 * 1024
FA_SZ = 6400


class KB:
    def __init__(self, taps=()):
        self.taps = set(taps)
        self.tap_handles = {}
        nc = self.nc = bass.Bass("TRN2", target_bir_lowering=False)
        self._ctx = []

        def din(name, shape):
            return nc.dram_tensor(name, shape, F32, kind="ExternalInput")

        def scrd(name, shape, dt):
            return nc.dram_tensor(name, shape, dt, kind="Internal")

        self.inp = {}
        for nm, shape in [
            ("x_slice", [TLOC, D]), ("halo_gate", [128, 1]),
            ("Wq", [D, D]), ("Wk", [D, D]), ("Wv", [D, D]),
            ("Wa", [D, NH]), ("Wb", [D, NH]), ("ba", [1, NH]), ("bb", [1, NH]),
            ("Wg", [D, D]), ("Wo", [D, D]), ("g_norm", [1, D]),
            ("slot_init", [NSLOT, D]), ("slot_scale", [NSLOT, 1]),
            ("w1", [HFF, 2 * D]), ("b1", [1, HFF]),
            ("w2", [NSLOT, HFF]), ("b2", [1, NSLOT]),
            ("gru_wih", [N3, D]), ("gru_whh", [N3, D]),
            ("gru_bih", [1, N3]), ("gru_bhh", [1, N3]),
            ("wv", [D, D]), ("bv", [1, D]), ("wo", [D, D]), ("bo", [1, D]),
            ("tau", [1, 1]),
        ]:
            self.inp[nm] = din(nm, shape)
        self.out = nc.dram_tensor("out", [TREAL, D], F32, kind="ExternalOutput")

        self.xn_dram = scrd("xn_dram", [TLOC, D], BF16)
        self.q_dram = scrd("q_dram", [TLOC, D], BF16)
        self.k_dram = scrd("k_dram", [TLOC, D], BF16)
        self.v_dram = scrd("v_dram", [TLOC, D], BF16)
        self.gate_dram = scrd("gate_dram", [TLOC, D], BF16)
        self.yf_dram = scrd("yf_dram", [TLOC, D], BF16)
        self.al_dram = scrd("al_dram", [TLOC, 128], BF16)
        self.wbf = {}
        for nm, shape in [("w1", [HFF, 2 * D]), ("w2", [NSLOT, HFF]),
                          ("wih", [N3, D]), ("whh", [N3, D]),
                          ("wv", [D, D]), ("wo", [D, D]), ("woa", [D, D]),
                          ("wg", [D, D]), ("wq", [D, D]), ("wk", [D, D]),
                          ("wvp", [D, D]), ("wab", [D, 2 * NH])]:
            self.wbf[nm] = scrd(nm + "_bfd", shape, BF16)
        self.gh_dram = scrd("gh_dram", [NSLOT, N3], BF16)
        self.cc_in = scrd("cc_in", [NSLOT, D + 16], F32)
        self.cc_out = scrd("cc_out", [NSLOT, D + 16], F32)
        self.lgc_dram = scrd("lgc_dram", [NT, NH], F32)

        sb = self._sb
        self.x_tm = sb("x_tm", [128, NT * D], F32)
        self.arena = sb("arena", [128, AR_SZ], BF16)
        self.farena = sb("farena", [128, FA_SZ], F32)
        self.ident_f = sb("ident_f", [128, 128], F32)
        self.ident_b = sb("ident_b", [128, 128], BF16)
        self.mask_incl = sb("mask_incl", [128, 128], F32)
        self.mask_strict = sb("mask_strict", [128, 128], F32)
        self.ones_col = sb("ones_col", [128, 1], BF16)
        self.ones_row = sb("ones_row", [1, 128], BF16)
        self.onesf_row = sb("onesf_row", [1, 128], F32)
        self.gate_col = sb("gate_col", [128, 1], F32)
        self.gateneg_col = sb("gateneg_col", [128, 1], F32)
        self.stats = sb("stats", [128, 128], F32)
        self.biasrows = sb("biasrows", [1, 3200], BF16)
        self.effb1_col = sb("effb1_col", [128, 4], F32)
        self.stau_col = sb("stau_col", [128, 1], F32)
        self.hs_slot = sb("hs_slot", [64, D], F32)
        self.hs_bf = sb("hs_bf", [64, D], BF16)
        self.smean_bf = sb("smean_bf", [128, 8], BF16)
        self.ab_tm = sb("ab_tm", [128, NT * 3 * NH], F32)
        self.lg_tm = sb("lg_tm", [128, NT * NH], F32)
        self.lgc_flat = sb("lgc_flat", [1, NT * NH], F32)
        self.M_all = sb("M_all", [128, (NH // 2) * HD], F32)
        self.M_bf = sb("M_bf", [128, (NH // 2) * HD], BF16)
        self.fmrow = sb("fmrow", [16, 256], F32)
        self.cvals = sb("cvals", [128, 16], F32)
        self._consts = [LN_EPS, -0.5, -1.0, 1.0, 0.5]
        for i, v in enumerate(self._consts):
            self.nc.const_aps.register(F32, v, self.cvals[:, i:i + 1])

        psf = lambda n: self._ps(n, [128, 512], F32)
        self.pA, self.pB, self.pC, self.pD = psf("pA"), psf("pB"), psf("pC"), psf("pD")
        self.pE, self.pF, self.pG = psf("pE"), psf("pF"), psf("pG")
        self.pH = self._ps("pH", [128, 1024], BF16)

        self.sem = {}
        for nm in ["sL", "sG", "sP", "sV", "sA", "sO", "sCC", "sScl"]:
            cm = nc.semaphore(nm)
            h = cm.__enter__()
            self._ctx.append(cm)
            self.sem[nm] = [h, 0]

    def _sb(self, name, shape, dt):
        cm = self.nc.sbuf_tensor(name, shape, dt)
        t = cm.__enter__()
        self._ctx.append(cm)
        return t

    def _ps(self, name, shape, dt):
        cm = self.nc.psum_tensor(name, shape, dt)
        t = cm.__enter__()
        self._ctx.append(cm)
        return t

    def inc(self, instr, sem, n=1):
        instr.then_inc(self.sem[sem][0], n)
        self.sem[sem][1] += n
        return instr

    def cnt(self, sem):
        return self.sem[sem][1]

    def wait(self, eng, sem, val=None):
        eng.wait_ge(self.sem[sem][0], self.cnt(sem) if val is None else val)

    def block(self, **bodies):
        with self.nc.Block() as blk:
            for en, fn in bodies.items():
                getattr(blk, en)(fn)

    def barrier(self):
        k = self
        self.block(tensor=lambda e: k.inc(e.nop(), "sP"),
                   vector=lambda e: k.inc(e.engine_nop(), "sV"),
                   scalar=lambda e: k.inc(e.nop(), "sA"),
                   gpsimd=lambda e: k.inc(e.engine_nop(), "sO"),
                   sync=lambda e: k.inc(e.nop(), "sScl"))
        tgt = {nm: self.cnt(nm) for nm in
               ["sP", "sV", "sA", "sO", "sScl", "sL", "sG", "sCC"]}

        def wall(e):
            for nm, c in tgt.items():
                e.wait_ge(self.sem[nm][0], c)
        self.block(tensor=wall, vector=wall, scalar=wall, gpsimd=wall, sync=wall)

    tap_shapes = {
        "xn1": ([TLOC, D], BF16), "h1": ([HFF, TLOC], BF16),
        "alpha1": ([TLOC, 128], BF16), "slotred1": ([NSLOT, D + 16], F32),
        "sv1": ([NSLOT, D], BF16), "x1": ([TLOC, D], F32),
        "qn": ([TLOC, D], BF16), "kn": ([TLOC, D], BF16),
        "vv": ([TLOC, D], BF16), "lgtm": ([128, NT * NH], F32),
        "yf": ([TLOC, D], BF16), "x2": ([TLOC, D], F32),
        "abtm": ([128, NT * 3 * NH], F32),
        "dbgab": ([128, 320], F32),
        "dbg": ([128, 320], F32),
        "dbg2": ([128, 8 * TLOC], BF16),
        "dbg3": ([128, 16 * 512], BF16),
    }

    def tapout(self, name, emit_fn):
        if name not in self.taps or name in self.tap_handles:
            return
        h = self.nc.dram_tensor("tap_" + name, self.tap_shapes[name][0],
                                self.tap_shapes[name][1], kind="ExternalOutput")
        self.tap_handles[name] = h
        self.barrier()
        emit_fn(h)
        self.barrier()

    def tap_dram(self, name, src):
        self.tapout(name, lambda hnd: self.block(
            gpsimd=lambda g: self.inc(g.dma_start(hnd.ap(), src.ap()), "sG", 16)))

    def tap_xtm(self, name, tiles):
        def emit(hnd):
            def g(g_):
                for i, t in enumerate(tiles):
                    self.inc(g_.dma_start(
                        bass.AP(hnd, i * 128 * D, [[D, 128], [1, D]]),
                        self.x_tm[:, t * D:(t + 1) * D]), "sG", 16)
            self.block(gpsimd=g)
        self.tapout(name, emit)

    def boot(self):
        k = self
        inp = self.inp
        FA, A = self.farena, self.arena
        wld = FA[:, 0:4096]
        wcast = A[:, 0:4096]

        def s_body(s):
            for t in range(NT):
                k.inc(s.dma_start(
                    k.x_tm[:, t * D:(t + 1) * D],
                    bass.AP(inp["x_slice"], t * 128 * D, [[D, 128], [1, D]])),
                    "sL", 16)
            k.inc(s.dma_start(k.gate_col[:], inp["halo_gate"].ap()), "sL", 16)

        def g_body(g):
            for i, v in enumerate(self._consts):
                g.memset(k.cvals[:, i:i + 1], v)
            k.inc(g.memset(k.ident_f[:], 0.0), "sO")
            k.wait(g, "sO")
            k.inc(g.affine_select(out=k.ident_f[:], in_=k.ident_f[:],
                                  compare_op=ALU.not_equal, fill=1.0, base=0,
                                  pattern=[[-1, 128]], channel_multiplier=1),
                  "sO")
            k.wait(g, "sO")
            g.tensor_copy(k.ident_b[:], k.ident_f[:])
            g.memset(k.ones_col[:], 1.0)
            g.memset(k.ones_row[:], 1.0)
            g.memset(k.onesf_row[:], 1.0)
            g.memset(k.M_all[:], 0.0)
            g.memset(k.M_bf[:], 0.0)
            g.iota(k.mask_incl[:], [[1, 128]], channel_multiplier=0,
                   allow_small_or_imprecise_dtypes=True)
            k.inc(g.iota(k.mask_strict[:], [[0, 128]], channel_multiplier=1,
                         allow_small_or_imprecise_dtypes=True), "sO")
            k.wait(g, "sO")
            k.inc(g.tensor_tensor(k.mask_incl[:], k.mask_incl[:],
                                  k.mask_strict[:], ALU.subtract), "sO")
            k.wait(g, "sO")
            k.inc(g.tensor_single_scalar(k.mask_strict[:], k.mask_incl[:], -0.5,
                                         ALU.is_gt), "sO")
            k.wait(g, "sO")
            k.inc(g.tensor_scalar_mul(k.mask_strict[:], k.mask_strict[:],
                                      -1e30), "sO")
            k.wait(g, "sO")
            k.inc(g.tensor_single_scalar(k.mask_incl[:], k.mask_incl[:], 0.5,
                                         ALU.is_gt), "sO")
            k.wait(g, "sO")
            k.inc(g.tensor_scalar_mul(k.mask_incl[:], k.mask_incl[:], -1e30),
                  "sO")
        self.block(sync=s_body, gpsimd=g_body)

        jobs = [
            ("w1", inp["w1"], HFF, 2 * D, None),
            ("w2", inp["w2"], NSLOT, HFF, None),
            ("wq", inp["Wq"], D, D, None),
            ("wk", inp["Wk"], D, D, None),
            ("wvp", inp["Wv"], D, D, None),
            ("wg", inp["Wg"], D, D, None),
            ("woa", inp["Wo"], D, D, None),
            ("wih", inp["gru_wih"], N3, D, None),
            ("whh", inp["gru_whh"], N3, D, None),
            ("wv", inp["wv"], D, D, None),
            ("wo", inp["wo"], D, D, 0.5),
        ]
        chunks = []
        for nm, src, rows, cols, scale in jobs:
            dram = self.wbf[nm]
            nch = (rows + 127) // 128
            for kc in range(nch):
                chunks.append((src, dram, kc, min(128, rows - kc * 128),
                               cols, scale))
        vc = []
        nchk = len(chunks)
        for i in range(nchk + 1):
            if i < nchk:
                src, dram, kc, rr, cols, scale = chunks[i]
                p = i % 2
                wl = FA[:, p * 2048:p * 2048 + 2048]
                wc = A[:, p * 2048:p * 2048 + 2048]

                def g1(g, i=i, src=src, kc=kc, rr=rr, cols=cols, wl=wl):
                    if i >= 2:
                        k.wait(g, "sV", vc[i - 2])
                    k.inc(g.dma_start(
                        wl[0:rr, 0:cols],
                        bass.AP(src, kc * 128 * cols, [[cols, rr], [1, cols]])),
                        "sG", 16)

                def v1(v, rr=rr, cols=cols, scale=scale, wl=wl, wc=wc):
                    k.wait(v, "sG")
                    if scale is None:
                        i_ = v.tensor_copy(wc[0:rr, 0:cols], wl[0:rr, 0:cols])
                    else:
                        i_ = v.tensor_scalar_mul(wc[0:rr, 0:cols],
                                                 wl[0:rr, 0:cols], scale)
                    k.inc(i_, "sV")
                self.block(gpsimd=g1, vector=v1)
                vc.append(self.cnt("sV"))
            if i >= 1:
                src, dram, kc, rr, cols, scale = chunks[i - 1]
                p = (i - 1) % 2
                wc = A[:, p * 2048:p * 2048 + 2048]

                def g2(g, i=i, dram=dram, kc=kc, rr=rr, cols=cols, wc=wc):
                    k.wait(g, "sV", vc[i - 1])
                    k.inc(g.dma_start(
                        bass.AP(dram, kc * 128 * cols, [[cols, rr], [1, cols]]),
                        wc[0:rr, 0:cols]), "sG", 16)
                self.block(gpsimd=g2)
        for kc in range(8):
            def g1(g, kc=kc):
                k.wait(g, "sV")
                k.inc(g.dma_start(
                    wld[:, 0:NH],
                    bass.AP(inp["Wa"], kc * 128 * NH, [[NH, 128], [1, NH]])),
                    "sG", 16)
                k.inc(g.dma_start(
                    wld[:, NH:2 * NH],
                    bass.AP(inp["Wb"], kc * 128 * NH, [[NH, 128], [1, NH]])),
                    "sG", 16)

            def v1(v):
                k.wait(v, "sG")
                k.inc(v.tensor_copy(wcast[:, 0:2 * NH], wld[:, 0:2 * NH]), "sV")

            def g2(g, kc=kc):
                k.wait(g, "sV")
                k.inc(g.dma_start(
                    bass.AP(self.wbf["wab"], kc * 128 * 2 * NH,
                            [[2 * NH, 128], [1, 2 * NH]]),
                    wcast[:, 0:2 * NH]), "sG", 16)
            self.block(gpsimd=g1, vector=v1)
            self.block(gpsimd=g2)
        self.barrier()

        self.BR = {}
        off = 0
        for nm, ln in [("b2", NSLOT), ("bv", D), ("halfbo", D), ("ba", NH),
                       ("bb", NH), ("gnorm", D), ("stau", 16)]:
            self.BR[nm] = (off, ln)
            off += ln
        assert off <= 3200
        rowf = FA[0:1, 0:3200]

        def g3(g):
            k.inc(g.memset(rowf[0:1, :], 0.0), "sO")
            k.wait(g, "sO")
            for nm, src in [("b2", inp["b2"]), ("bv", inp["bv"]),
                            ("halfbo", inp["bo"]), ("ba", inp["ba"]),
                            ("bb", inp["bb"]), ("gnorm", inp["g_norm"])]:
                o, ln = self.BR[nm]
                k.inc(g.dma_start(rowf[0:1, o:o + ln], src.ap()), "sG", 16)
            o = self.BR["stau"][0]
            k.inc(g.dma_start(rowf[0:1, o:o + 1], inp["tau"].ap()), "sG", 16)
            k.inc(g.dma_start(FA[0:64, 4096:4096 + D], inp["slot_init"].ap()),
                  "sG", 16)
            k.inc(g.dma_start(FA[0:64, 5120:5121],
                              bass.AP(inp["slot_scale"], 0, [[1, 64], [1, 1]])),
                  "sG", 16)

        def a3x(a):
            k.wait(a, "sG")
            o = self.BR["stau"][0]
            a.mul(rowf[0:1, o + 4:o + 5], rowf[0:1, o:o + 1], -1.0)
            k.inc(a.nop(), "sA")

        def v3(v):
            k.wait(v, "sA")
            o = self.BR["stau"][0]
            k.inc(v.tensor_tensor(rowf[0:1, o + 1:o + 2], rowf[0:1, o:o + 1],
                                  rowf[0:1, o + 4:o + 5], ALU.max), "sV")
            o = self.BR["halfbo"][0]
            v.tensor_scalar_mul(rowf[0:1, o:o + D], rowf[0:1, o:o + D], 0.5)
            o = self.BR["stau"][0]
            k.wait(v, "sV")
            k.inc(v.tensor_scalar_add(rowf[0:1, o + 2:o + 3],
                                      rowf[0:1, o + 1:o + 2], 0.1), "sV")
            k.wait(v, "sV")
            k.inc(v.reciprocal(rowf[0:1, o + 3:o + 4], rowf[0:1, o + 2:o + 3]),
                  "sV")
            k.wait(v, "sV")
            v.tensor_copy(k.biasrows[0:1, :], rowf[0:1, :])
            k.inc(v.tensor_scalar(k.hs_slot[:], FA[0:64, 4096:4096 + D],
                                  FA[0:64, 5120:5121], None, ALU.mult), "sV")
            k.wait(v, "sV")
            k.inc(v.tensor_copy(k.hs_bf[:], k.hs_slot[:]), "sV")
        self.block(gpsimd=g3, scalar=a3x)
        self.block(vector=v3)
        self.barrier()

        o = self.BR["stau"][0]

        def t4(t):
            k.inc(t.matmul(k.pA[:, 0:1], k.ones_row[0:1, :],
                           k.biasrows[0:1, o + 3:o + 4], start=True, stop=True),
                  "sP")
            for kc in range(8):
                k.inc(t.matmul(k.pB[:, kc:kc + 1],
                               k.hs_bf[:, kc * 128:(kc + 1) * 128],
                               k.ones_col[0:64, :], start=True, stop=True), "sP")

        def v4(v):
            k.wait(v, "sP")
            v.tensor_scalar_mul(k.gateneg_col[:], k.gate_col[:], -1.0)
            v.tensor_copy(k.stau_col[:], k.pA[:, 0:1])
            k.inc(v.tensor_scalar_mul(k.smean_bf[:], k.pB[:, 0:8], 1.0 / NSLOT),
                  "sV")
        self.block(tensor=t4)
        self.block(vector=v4)
        self.barrier()

    def gh_precompute(self):
        k = self
        A, FA = self.arena, self.farena
        HSF = 8192
        BH = HSF + 512
        GRUB = BH + N3 + 64

        def t1(t):
            for kc in range(8):
                t.transpose(k.pH[:, kc * 64:(kc + 1) * 64],
                            k.hs_bf[:, kc * 128:(kc + 1) * 128],
                            k.ident_b[0:64, 0:64])
            k.inc(t.nop(), "sP")

        def v1(v):
            k.wait(v, "sP")
            k.inc(v.tensor_copy(A[:, HSF:HSF + 512], k.pH[:, 0:512]), "sV")
        self.block(tensor=t1)
        self.block(vector=v1)

        def g1(g):
            k.inc(g.dma_start(FA[0:1, 0:N3], self.inp["gru_bhh"].ap()), "sG", 16)

        def v2(v):
            k.wait(v, "sG")
            k.inc(v.tensor_copy(A[0:1, BH:BH + N3], FA[0:1, 0:N3]), "sV")
        self.block(gpsimd=g1, vector=v2)

        BANK6 = [k.pA, k.pB, k.pC, k.pD, k.pE, k.pF]
        for kc in range(8):
            def s1(s, kc=kc):
                if kc >= 2:
                    k.wait(s, "sP", self._gh_p0 + (kc - 1))
                k.inc(s.dma_start(
                    A[:, GRUB + (kc % 2) * N3:GRUB + (kc % 2) * N3 + N3],
                    bass.AP(self.wbf["whh"], kc * 128, [[D, N3], [1, 128]]),
                    transpose=True), "sL", 16)
            self.block(sync=s1)

            def t2(t, kc=kc):
                k.wait(t, "sL")
                if kc == 0:
                    k.wait(t, "sV")
                for n3c in range(6):
                    i = t.matmul(
                        BANK6[n3c][0:64, 0:512],
                        A[:, HSF + kc * 64:HSF + (kc + 1) * 64],
                        A[:, GRUB + (kc % 2) * N3 + n3c * 512:
                          GRUB + (kc % 2) * N3 + (n3c + 1) * 512],
                        start=(kc == 0), stop=False)
                if kc == 7:
                    for n3c in range(6):
                        i = t.matmul(
                            BANK6[n3c][0:64, 0:512], k.ones_row[0:1, 0:64],
                            A[0:1, BH + n3c * 512:BH + (n3c + 1) * 512],
                            start=False, stop=True)
                k.inc(i, "sP")
            if kc == 0:
                self._gh_p0 = self.cnt("sP")
            self.block(tensor=t2)

        def v3(v):
            k.wait(v, "sP")
            for n3c in range(6):
                i = v.tensor_copy(A[0:64, HSF + n3c * 512:HSF + (n3c + 1) * 512],
                                  BANK6[n3c][0:64, 0:512])
            k.inc(i, "sV")

        def g3(g):
            k.wait(g, "sV")
            k.inc(g.dma_start(self.gh_dram.ap(), A[0:64, HSF:HSF + N3]), "sG", 16)
        self.block(vector=v3)
        self.block(gpsimd=g3)
        self.barrier()

    def ln(self, tiles):
        k = self
        ntl = len(tiles)
        ssum = k.stats[:, 0:ntl]
        ssq = k.stats[:, 24:24 + ntl]
        mu = k.stats[:, 48:48 + ntl]
        tmp = k.stats[:, 72:72 + ntl]
        rstd = k.stats[:, 96:96 + ntl]
        sqs = k.farena[:, 0:D]
        xnb = [k.arena[:, 4096 + i * D:4096 + (i + 1) * D] for i in range(3)]

        def v1(v):
            for j, t in enumerate(tiles):
                v.reduce_sum(ssum[:, j:j + 1], k.x_tm[:, t * D:(t + 1) * D],
                             axis=AX.X)
            k.inc(v.engine_nop(), "sV")

        def a1(a):
            for j, t in enumerate(tiles):
                k.inc(a.activation(sqs[:], k.x_tm[:, t * D:(t + 1) * D],
                                   AF.Square, accum_out=ssq[:, j:j + 1]), "sA")
                k.wait(a, "sA")
            k.inc(a.nop(), "sA")
        self.block(vector=v1, scalar=a1)

        def v2(v):
            k.wait(v, "sA")
            k.inc(v.tensor_scalar_mul(mu[:], ssum[:], 1.0 / D), "sV")
        self.block(vector=v2)

        def a2(a):
            k.wait(a, "sV")
            k.inc(a.activation(tmp[:], mu[:], AF.Square), "sA")
        self.block(scalar=a2)

        def v3(v):
            k.wait(v, "sA")
            k.inc(v.scalar_tensor_tensor(rstd[:], ssq[:], 1.0 / D, tmp[:],
                                         ALU.mult, ALU.subtract), "sV")
        self.block(vector=v3)

        def a3(a):
            k.wait(a, "sV")
            k.inc(a.activation(tmp[:], rstd[:], AF.Ln, bias=LN_EPS), "sA")
            k.wait(a, "sA")
            k.inc(a.activation(rstd[:], tmp[:], AF.Exp, scale=-0.5), "sA")
        self.block(scalar=a3)

        g0 = [None]
        for j, t in enumerate(tiles):
            buf = xnb[j % 3]

            def v4(v, j=j, t=t, buf=buf):
                if j == 0:
                    k.wait(v, "sA")
                if j >= 3:
                    k.wait(v, "sG", g0[0] + 16 * (j - 2))
                k.inc(v.tensor_scalar(buf, k.x_tm[:, t * D:(t + 1) * D],
                                      mu[:, j:j + 1], rstd[:, j:j + 1],
                                      ALU.subtract, ALU.mult), "sV")

            def g4(g, t=t, buf=buf):
                k.wait(g, "sV")
                k.inc(g.dma_start(
                    bass.AP(self.xn_dram, t * 128 * D, [[D, 128], [1, D]]),
                    buf), "sG", 16)
            if j == 0:
                g0[0] = self.cnt("sG")
            self.block(vector=v4, gpsimd=g4)
        self.barrier()

    def xnfm_load(self, src_dram, dst_off, rows=TLOC):
        k = self

        def s1(s):
            for kc in range(8):
                k.inc(s.dma_start(
                    k.arena[:, dst_off + kc * rows:dst_off + (kc + 1) * rows],
                    bass.AP(src_dram, kc * 128, [[D, rows], [1, 128]]),
                    transpose=True), "sL", 16)
        self.block(sync=s1)

    def ffn(self, tiles, tag):
        k = self
        A, FA = self.arena, self.farena
        rows = TLOC
        real = [t for t in tiles if t >= HT]
        tcols = [(i * 512, min(512, rows - i * 512))
                 for i in range((rows + 511) // 512)]

        XN = 0
        W1T = XN + 8 * rows
        W2T = W1T + 8192
        HFM = W2T + 256
        XNT = 0
        ALB = XNT + 3 * 1032
        assert HFM + 4 * rows <= AR_SZ

        def s1(s):
            for kc in range(8):
                k.inc(s.dma_start(
                    A[:, XN + kc * rows:XN + (kc + 1) * rows],
                    bass.AP(self.xn_dram, kc * 128, [[D, rows], [1, 128]]),
                    transpose=True), "sL", 16)
            for kc in range(16):
                k.inc(s.dma_start(
                    A[:, W1T + kc * 512:W1T + (kc + 1) * 512],
                    bass.AP(self.wbf["w1"], kc * 128, [[2 * D, HFF], [1, 128]]),
                    transpose=True), "sL", 16)
            for kc in range(4):
                k.inc(s.dma_start(
                    A[:, W2T + kc * 64:W2T + (kc + 1) * 64],
                    bass.AP(self.wbf["w2"], kc * 128, [[HFF, NSLOT], [1, 128]]),
                    transpose=True), "sL", 16)
        self.block(sync=s1)

        if tag == 1:
            def emit_dbg2(hnd):
                self.block(gpsimd=lambda g: self.inc(g.dma_start(
                    bass.AP(hnd, 0, [[8 * rows, 128], [1, 8 * rows]]),
                    A[:, XN:XN + 8 * rows]), "sG", 16))
            self.tapout("dbg2", emit_dbg2)
            def emit_dbg3(hnd):
                self.block(gpsimd=lambda g: self.inc(g.dma_start(
                    bass.AP(hnd, 0, [[16 * 512, 128], [1, 16 * 512]]),
                    A[:, W1T:W1T + 16 * 512]), "sG", 16))
            self.tapout("dbg3", emit_dbg3)

        def g1(g):
            k.inc(g.dma_start(FA[0:1, 0:HFF], self.inp["b1"].ap()), "sG", 16)
        self.block(gpsimd=g1)

        def t1(t):
            k.wait(t, "sL")
            k.wait(t, "sG")
            for hc in range(4):
                for kc in range(8):
                    t.matmul(k.pE[:, hc:hc + 1],
                             A[:, W1T + (8 + kc) * 512 + hc * 128:
                               W1T + (8 + kc) * 512 + (hc + 1) * 128],
                             k.smean_bf[:, kc:kc + 1],
                             start=(kc == 0), stop=(kc == 7))
            for hc in range(4):
                t.transpose(k.pF[:, hc:hc + 1], FA[0:1, hc * 128:(hc + 1) * 128],
                            k.ident_f[0:1, 0:1])
            k.inc(t.nop(), "sP")
        self.block(tensor=t1)

        def a1(a):
            k.wait(a, "sP")
            k.inc(a.copy(FA[:, 208:212], k.pF[:, 0:4]), "sA")
        self.block(scalar=a1)

        def v1(v):
            k.wait(v, "sA")
            k.inc(v.tensor_tensor(k.effb1_col[:], k.pE[:, 0:4], FA[:, 208:212],
                                  ALU.add), "sV")
        self.block(vector=v1)
        self.barrier()

        pAB = [k.pA, k.pB]
        nmm = 0
        a0 = [None]
        for hc in range(4):
            for (c0, cw) in tcols:
                bank = pAB[nmm % 2]

                def tmm(t, hc=hc, c0=c0, cw=cw, bank=bank, nmm=nmm):
                    if nmm >= 2:
                        k.wait(t, "sA", a0[0] + (nmm - 1))
                    for kc in range(8):
                        i = t.matmul(
                            bank[:, 0:cw],
                            A[:, W1T + kc * 512 + hc * 128:
                              W1T + kc * 512 + (hc + 1) * 128],
                            A[:, XN + kc * rows + c0:XN + kc * rows + c0 + cw],
                            start=(kc == 0), stop=(kc == 7))
                    k.inc(i, "sP")

                def act(a, hc=hc, c0=c0, cw=cw, bank=bank):
                    k.wait(a, "sP")
                    k.inc(a.activation(
                        A[:, HFM + hc * rows + c0:HFM + hc * rows + c0 + cw],
                        bank[:, 0:cw], AF.Gelu,
                        bias=k.effb1_col[:, hc:hc + 1]), "sA")
                if nmm == 0:
                    a0[0] = self.cnt("sA")
                self.block(tensor=tmm, scalar=act)
                nmm += 1
        self.barrier()
        if tag == 1:
            def emit_h1(hnd):
                def g(g_):
                    for hc in range(4):
                        self.inc(g_.dma_start(
                            bass.AP(hnd, hc * 128 * rows, [[rows, 128], [1, rows]]),
                            A[:, HFM + hc * rows:HFM + (hc + 1) * rows]),
                            "sG", 16)
                self.block(gpsimd=g)
            self.tapout("h1", emit_h1)

        o_b2 = self.BR["b2"][0]
        m8 = FA[:, 0:8]
        mneg = FA[:, 8:9]
        den = FA[:, 9:10]
        rden = FA[:, 10:11]
        maskt = FA[:, 16:80]
        et = FA[:, 80:144]
        lgsb = FA[:, 144:208]
        aun = FA[:, 256:320]
        pool_first = [True]
        aexp = []
        vfin = []
        xld = {}
        pcnt = []

        for j, t in enumerate(tiles):
            xbuf = A[:, XNT + (j % 3) * 1032:XNT + (j % 3) * 1032 + 1032]
            if t in real:
                def s2(s, t=t, j=j, xbuf=xbuf):
                    ridx = real.index(t)
                    if ridx >= 3:
                        k.wait(s, "sP", pcnt[ridx - 3])
                    k.inc(s.dma_start(
                        xbuf[:, 0:D],
                        bass.AP(self.xn_dram, t * 128 * D, [[D, 128], [1, D]])),
                        "sL", 16)
                    xld[t] = self.cnt("sL")
                self.block(sync=s2)
                if len(xld) <= 3:
                    def g2(g, xbuf=xbuf):
                        g.memset(xbuf[:, 1025:1032], 0.0)
                        k.inc(g.memset(xbuf[:, 1024:1025], 1.0), "sO")
                    self.block(gpsimd=g2)

            def t3(t_, t=t, j=j):
                if j == 0:
                    k.wait(t_, "sO")
                if j >= 1:
                    k.wait(t_, "sA", aexp[j - 1])
                for hc in range(4):
                    t_.matmul(k.pC[:, 0:64],
                              A[:, HFM + hc * rows + t * 128:
                                HFM + hc * rows + (t + 1) * 128],
                              A[:, W2T + hc * 64:W2T + (hc + 1) * 64],
                              start=(hc == 0), stop=False)
                k.inc(t_.matmul(k.pC[:, 0:64], k.ones_row[0:1, :],
                                k.biasrows[0:1, o_b2:o_b2 + NSLOT],
                                start=False, stop=True), "sP")
            self.block(tensor=t3)

            def v3(v, j=j):
                k.wait(v, "sP")
                if j >= 1:
                    k.wait(v, "sA", aexp[j - 1])
                k.inc(v.tensor_scalar(lgsb[:], k.pC[:, 0:64],
                                      k.stau_col[:, 0:1], None, ALU.mult), "sV")
                k.wait(v, "sV")
                k.inc(v.max(m8[:], lgsb[:]), "sV")
                k.wait(v, "sV")
                v.tensor_scalar(mneg[:], m8[:, 0:1], -1.0, None, ALU.mult)
                v.tensor_scalar(maskt[:], lgsb[:], m8[:, 7:8], -1e30,
                                ALU.is_lt, ALU.mult)
                k.inc(v.tensor_tensor(aun[:], lgsb[:], maskt[:], ALU.add),
                      "sV")
            self.block(vector=v3)

            def a3(a):
                k.wait(a, "sV")
                k.inc(a.activation(et[:], aun[:], AF.Exp,
                                   bias=mneg[:, 0:1], accum_out=den[:]), "sA")
            self.block(scalar=a3)
            aexp.append(self.cnt("sA"))

            albuf = A[:, ALB + (j % 3) * 64:ALB + (j % 3) * 64 + 64]

            def v4(v, albuf=albuf, j=j):
                k.wait(v, "sA")
                if j >= 3:
                    k.wait(v, "sG", self._al_g[j - 3])
                    if tiles[j - 3] in real:
                        k.wait(v, "sP", pcnt[real.index(tiles[j - 3])])
                k.inc(v.reciprocal(rden[:], den[:]), "sV")
                k.wait(v, "sV")
                k.inc(v.tensor_scalar(albuf, et[:], rden[:, 0:1], None,
                                      ALU.mult), "sV")
            if j == 0:
                self._al_g = []
            self.block(vector=v4)
            vfin.append(self.cnt("sV"))
            if j == 1 and tag == 1 and "dbg" in self.taps:
                def emit_dbg(hnd):
                    def vv(v):
                        v.tensor_copy(FA[:, 12:13], k.stau_col[:])
                        o_st = self.BR["stau"][0]
                        v.tensor_copy(FA[0:1, 300:308],
                                      k.biasrows[0:1, o_st:o_st + 8])
                        k.inc(v.tensor_copy(FA[:, 13:14], k.effb1_col[:, 0:1]),
                              "sV")
                    self.block(vector=vv)
                    def gg(g):
                        k.wait(g, "sV")
                        self.inc(g.dma_start(
                            bass.AP(hnd, 0, [[320, 128], [1, 320]]),
                            FA[:, 0:320]), "sG", 16)
                    self.block(gpsimd=gg)
                self.tapout("dbg", emit_dbg)

            def g5(g, t=t, albuf=albuf):
                k.wait(g, "sV")
                k.inc(g.dma_start(
                    bass.AP(self.al_dram, t * 128 * 128, [[128, 128], [1, 64]]),
                    albuf), "sG", 16)
            self.block(gpsimd=g5)
            self._al_g.append(self.cnt("sG"))

            if t in real:
                xbuf = A[:, XNT + (j % 3) * 1032:XNT + (j % 3) * 1032 + 1032]
                first = pool_first[0]
                pool_first[0] = False

                def t5(t_, t=t, xbuf=xbuf, albuf=albuf, first=first):
                    k.wait(t_, "sV", vfin[-1])
                    k.wait(t_, "sL", xld[t])
                    k.wait(t_, "sO")
                    t_.matmul(k.pF[0:64, 0:512], albuf, xbuf[:, 0:512],
                              start=first, stop=False)
                    t_.matmul(k.pG[0:64, 0:512], albuf, xbuf[:, 512:1024],
                              start=first, stop=False)
                    k.inc(t_.matmul(k.pE[0:64, 8:16], albuf, xbuf[:, 1024:1032],
                                    start=first, stop=False), "sP")
                self.block(tensor=t5)
                pcnt.append(self.cnt("sP"))
            if getattr(self, "_paranoid", False):
                self.barrier()
        self.barrier()
        if tag == 1:
            self.tap_dram("alpha1", self.al_dram)

        slotbuf = FA[0:64, 0:D + 16]

        def v6(v):
            v.tensor_copy(slotbuf[:, 0:512], k.pF[0:64, 0:512])
            v.tensor_copy(slotbuf[:, 512:1024], k.pG[0:64, 0:512])
            k.inc(v.tensor_copy(slotbuf[:, 1024:1032], k.pE[0:64, 8:16]), "sV")
        self.block(vector=v6)

        def g6(g):
            k.wait(g, "sV")
            k.inc(g.dma_start(self.cc_in.ap(), slotbuf[:, 0:D + 16]), "sG", 16)
            k.wait(g, "sG")
            k.inc(g.collective_compute(
                "AllReduce", ALU.add, [[0, 1, 2, 3], [4, 5, 6, 7]],
                [self.cc_in.ap().opt()], [self.cc_out.ap().opt()]), "sCC")
            k.wait(g, "sCC")
            k.inc(g.dma_start(slotbuf[:, 0:D + 16], self.cc_out.ap()), "sG", 16)
        self.block(gpsimd=g6)
        self.barrier()
        if tag == 1:
            def emit_sr(hnd):
                self.block(gpsimd=lambda g: self.inc(
                    g.dma_start(hnd.ap(), slotbuf[:, 0:D + 16]), "sG", 16))
            self.tapout("slotred1", emit_sr)

        GRU = 0
        GIF = GRU + 2 * N3
        BIH = GIF + 512
        GIB = BIH + N3 + 64
        SVB = GIB + 1024
        SMA = SVB + 1024
        SOF = SMA + rows
        assert SOF + 8 * rows <= AR_SZ

        GHB = SOF + 8 * rows
        assert GHB + N3 <= AR_SZ
        gi = FA[0:64, 2048:2048 + D]
        gx = FA[0:64, 3072:3072 + N3]
        ghb = A[0:64, GHB:GHB + N3]
        wk1 = FA[0:64, 0:D]
        wk2 = FA[0:64, 1024:1024 + D]
        wk3 = FA[0:64, 2048:2048 + D]
        gib = A[:, GIB:GIB + D]
        gifm = A[:, GIF:GIF + 512]

        def v8(v):
            k.inc(v.tensor_scalar_add(slotbuf[:, 1024:1025],
                                      slotbuf[:, 1024:1025], 1e-8), "sV")
            k.wait(v, "sV")
            k.inc(v.reciprocal(slotbuf[:, 1026:1027], slotbuf[:, 1024:1025]),
                  "sV")
            k.wait(v, "sV")
            v.tensor_scalar(gi[:], slotbuf[:, 0:D], slotbuf[:, 1026:1027],
                            None, ALU.mult)
            k.inc(v.tensor_copy(gib[0:64, 0:D], gi[:]), "sV")
        self.block(vector=v8)

        def t9(t):
            k.wait(t, "sV")
            for kc in range(8):
                t.transpose(k.pH[:, kc * 64:(kc + 1) * 64],
                            gib[0:64, kc * 128:(kc + 1) * 128],
                            k.ident_b[0:64, 0:64])
            k.inc(t.nop(), "sP")
        self.block(tensor=t9)

        def vA(v):
            k.wait(v, "sP")
            k.inc(v.tensor_copy(gifm[:], k.pH[:, 0:512]), "sV")
        self.block(vector=vA)

        def gB(g):
            k.inc(g.dma_start(ghb, self.gh_dram.ap()), "sG", 16)
            k.inc(g.dma_start(FA[0:1, 0:N3], self.inp["gru_bih"].ap()), "sG", 16)
        self.block(gpsimd=gB)

        def vB(v):
            k.wait(v, "sG")
            k.inc(v.tensor_copy(A[0:1, BIH:BIH + N3], FA[0:1, 0:N3]), "sV")
        self.block(vector=vB)

        BANK6 = [k.pA, k.pB, k.pC, k.pD, k.pE, k.pF]
        p0 = [None]
        for kc in range(8):
            def sC(s, kc=kc):
                if kc >= 2:
                    k.wait(s, "sP", p0[0] + (kc - 1))
                k.inc(s.dma_start(
                    A[:, GRU + (kc % 2) * N3:GRU + (kc % 2) * N3 + N3],
                    bass.AP(self.wbf["wih"], kc * 128, [[D, N3], [1, 128]]),
                    transpose=True), "sL", 16)
            self.block(sync=sC)

            def tC(t, kc=kc):
                k.wait(t, "sL")
                if kc == 0:
                    k.wait(t, "sV")
                for n3c in range(6):
                    i = t.matmul(
                        BANK6[n3c][0:64, 0:512],
                        gifm[:, kc * 64:(kc + 1) * 64],
                        A[:, GRU + (kc % 2) * N3 + n3c * 512:
                          GRU + (kc % 2) * N3 + (n3c + 1) * 512],
                        start=(kc == 0), stop=False)
                if kc == 7:
                    for n3c in range(6):
                        i = t.matmul(
                            BANK6[n3c][0:64, 0:512], k.ones_row[0:1, 0:64],
                            A[0:1, BIH + n3c * 512:BIH + (n3c + 1) * 512],
                            start=False, stop=True)
                k.inc(i, "sP")
            if kc == 0:
                p0[0] = self.cnt("sP")
            self.block(tensor=tC)

        def vD(v):
            k.wait(v, "sP")
            for n3c in range(6):
                i = v.tensor_copy(gx[:, n3c * 512:(n3c + 1) * 512],
                                  BANK6[n3c][0:64, 0:512])
            k.inc(i, "sV")
        self.block(vector=vD)

        def vF(v):
            k.inc(v.tensor_tensor(wk1[:], gx[:, 0:D], ghb[:, 0:D], ALU.add),
                  "sV")
        self.block(vector=vF)

        def aF(a):
            k.wait(a, "sV")
            k.inc(a.activation(wk3[:], wk1[:], AF.Sigmoid), "sA")
        self.block(scalar=aF)

        def vG(v):
            k.wait(v, "sA")
            v.tensor_tensor(wk1[:], wk3[:], ghb[:, 2 * D:3 * D], ALU.mult)
            v.tensor_tensor(wk1[:], wk1[:], gx[:, 2 * D:3 * D], ALU.add)
            k.inc(v.tensor_tensor(wk2[:], gx[:, D:2 * D], ghb[:, D:2 * D],
                                  ALU.add), "sV")
        self.block(vector=vG)

        def aG(a):
            k.wait(a, "sV")
            a.activation(wk3[:], wk1[:], AF.Tanh)
            k.inc(a.activation(gx[:, 0:D], wk2[:], AF.Sigmoid), "sA")
        self.block(scalar=aG)

        def vH(v):
            k.wait(v, "sA")
            v.tensor_tensor(wk1[:], k.hs_slot[:], wk3[:], ALU.subtract)
            v.tensor_tensor(wk1[:], wk1[:], gx[:, 0:D], ALU.mult)
            v.tensor_tensor(gx[:, D:2 * D], wk3[:], wk1[:], ALU.add)
            k.inc(v.tensor_copy(gib[0:64, 0:D], gx[:, D:2 * D]), "sV")
        self.block(vector=vH)

        def tI(t):
            k.wait(t, "sV")
            for kc in range(8):
                t.transpose(k.pH[:, kc * 64:(kc + 1) * 64],
                            gib[0:64, kc * 128:(kc + 1) * 128],
                            k.ident_b[0:64, 0:64])
            k.inc(t.nop(), "sP")
        self.block(tensor=tI)

        def vI(v):
            k.wait(v, "sP")
            k.inc(v.tensor_copy(gifm[:], k.pH[:, 0:512]), "sV")
        self.block(vector=vI)

        WVT = GRU

        def sJ(s):
            k.wait(s, "sP")
            for kc in range(8):
                k.inc(s.dma_start(
                    A[:, WVT + kc * 768:WVT + kc * 768 + 768],
                    bass.AP(self.wbf["wv"], kc * 128, [[D, D], [1, 128]]),
                    transpose=True) if False else s.dma_start(
                    A[:, WVT + kc * D:WVT + (kc + 1) * D] if kc < 6 else
                    A[:, GIB + 1024 + (kc - 6) * D: GIB + 1024 + (kc - 5) * D],
                    bass.AP(self.wbf["wv"], kc * 128, [[D, D], [1, 128]]),
                    transpose=True), "sL", 16)
        WVT2 = SOF

        def sJ2(s):
            for kc in range(8):
                k.inc(s.dma_start(
                    A[:, WVT2 + kc * D:WVT2 + (kc + 1) * D],
                    bass.AP(self.wbf["wv"], kc * 128, [[D, D], [1, 128]]),
                    transpose=True), "sL", 16)
        self.block(sync=sJ2)

        o_bv = self.BR["bv"][0]
        svb = A[:, SVB:SVB + D]

        def tK(t):
            k.wait(t, "sL")
            k.wait(t, "sV")
            for half, bank in ((0, k.pA), (1, k.pB)):
                for kc in range(8):
                    t.matmul(bank[0:64, 0:512],
                             gifm[:, kc * 64:(kc + 1) * 64],
                             A[:, WVT2 + kc * D + half * 512:
                               WVT2 + kc * D + (half + 1) * 512],
                             start=(kc == 0), stop=False)
                k.inc(t.matmul(
                    bank[0:64, 0:512], k.ones_row[0:1, 0:64],
                    k.biasrows[0:1, o_bv + half * 512:o_bv + (half + 1) * 512],
                    start=False, stop=True), "sP")
        self.block(tensor=tK)

        def vK(v):
            k.wait(v, "sP")
            v.tensor_copy(svb[0:64, 0:512], k.pA[0:64, 0:512])
            k.inc(v.tensor_copy(svb[0:64, 512:1024], k.pB[0:64, 0:512]), "sV")
        self.block(vector=vK)
        self.barrier()
        if tag == 1:
            def emit_sv(hnd):
                self.block(gpsimd=lambda g: self.inc(
                    g.dma_start(hnd.ap(), svb[0:64, 0:D]), "sG", 16))
            self.tapout("sv1", emit_sv)

        def sL_(s):
            k.inc(s.dma_start(
                A[:, SMA:SMA + rows],
                bass.AP(self.al_dram, 0, [[128, rows], [1, 128]]),
                transpose=True), "sL", 16)
        self.block(sync=sL_)

        nso = 0
        soa0 = [None]
        for jc in range(8):
            for (c0, cw) in tcols:
                bank = pAB[nso % 2]

                def tso(t, jc=jc, c0=c0, cw=cw, bank=bank, nso=nso):
                    if nso == 0:
                        k.wait(t, "sL")
                    if nso >= 2:
                        k.wait(t, "sA", soa0[0] + (nso - 1))
                    k.inc(t.matmul(bank[:, 0:cw],
                                   svb[0:64, jc * 128:(jc + 1) * 128],
                                   A[0:64, SMA + c0:SMA + c0 + cw],
                                   start=True, stop=True), "sP")

                def aso(a, jc=jc, c0=c0, cw=cw, bank=bank):
                    k.wait(a, "sP")
                    k.inc(a.copy(
                        A[:, SOF + jc * rows + c0:SOF + jc * rows + c0 + cw],
                        bank[:, 0:cw]), "sA")
                if nso == 0:
                    soa0[0] = self.cnt("sA")
                self.block(tensor=tso, scalar=aso)
                nso += 1
        self.barrier()

        WOT = GRU
        WOT = 0
        assert 8 * D <= SMA

        def sM(s):
            for kc in range(8):
                k.inc(s.dma_start(
                    A[:, WOT + kc * D:WOT + (kc + 1) * D],
                    bass.AP(self.wbf["wo"], kc * 128, [[D, D], [1, 128]]),
                    transpose=True), "sL", 16)
        self.block(sync=sM)

        o_bo = self.BR["halfbo"][0]
        v0 = [None]
        for j, t in enumerate(tiles):
            bank0, bank1 = (k.pA, k.pB) if j % 2 == 0 else (k.pC, k.pD)

            def tN(t_, t=t, j=j, bank0=bank0, bank1=bank1):
                if j == 0:
                    k.wait(t_, "sL")
                if j >= 2:
                    k.wait(t_, "sV", v0[0] + (j - 1))
                for half, bank in ((0, bank0), (1, bank1)):
                    for kc in range(8):
                        t_.matmul(bank[:, 0:512],
                                  A[:, SOF + kc * rows + t * 128:
                                    SOF + kc * rows + (t + 1) * 128],
                                  A[:, WOT + kc * D + half * 512:
                                    WOT + kc * D + (half + 1) * 512],
                                  start=(kc == 0), stop=False)
                    k.inc(t_.matmul(
                        bank[:, 0:512], k.ones_row[0:1, :],
                        k.biasrows[0:1, o_bo + half * 512:o_bo + (half + 1) * 512],
                        start=False, stop=True), "sP")
            self.block(tensor=tN)

            def vN(v, t=t, bank0=bank0, bank1=bank1):
                k.wait(v, "sP")
                v.tensor_tensor(k.x_tm[:, t * D:t * D + 512],
                                k.x_tm[:, t * D:t * D + 512],
                                bank0[:, 0:512], ALU.add)
                k.inc(v.tensor_tensor(k.x_tm[:, t * D + 512:(t + 1) * D],
                                      k.x_tm[:, t * D + 512:(t + 1) * D],
                                      bank1[:, 0:512], ALU.add), "sV")
            if j == 0:
                v0[0] = self.cnt("sV")
            self.block(vector=vN)
        self.barrier()


    def attn_proj(self):
        k = self
        A, FA = self.arena, self.farena
        rows = TLOC
        AXN = 0
        AW = AXN + 8 * rows
        AWAB = AW + 8 * D
        AOB = AWAB + 8 * 2 * NH
        assert AOB + 2 * D <= AR_SZ

        self.xnfm_load(self.xn_dram, AXN)
        self.barrier()

        sq = FA[:, 0:D]
        ss = FA[:, 1024:1040]
        sn = FA[:, 1040:1056]
        rn = FA[:, 1056:1072]

        for wnm, dest, tl, epi in [
            ("wq", self.q_dram, list(range(HT, NT)), "l2"),
            ("wk", self.k_dram, list(range(NT)), "l2"),
            ("wvp", self.v_dram, list(range(NT)), "copy"),
            ("wg", self.gate_dram, list(range(HT, NT)), "silu"),
        ]:
            def sW(s, wnm=wnm):
                for kc in range(8):
                    k.inc(s.dma_start(
                        A[:, AW + kc * D:AW + (kc + 1) * D],
                        bass.AP(self.wbf[wnm], kc * 128 * D, [[D, 128], [1, D]])),
                        "sL", 16)
            self.block(sync=sW)
            vprev = [None]
            for j, t in enumerate(tl):
                bank0, bank1 = (k.pA, k.pB) if j % 2 == 0 else (k.pC, k.pD)
                obuf = A[:, AOB + (j % 2) * D:AOB + (j % 2) * D + D]

                def tP(t_, t=t, j=j, bank0=bank0, bank1=bank1):
                    if j == 0:
                        k.wait(t_, "sL")
                    if j >= 2:
                        k.wait(t_, "sV", vprev[0])
                    for half, bank in ((0, bank0), (1, bank1)):
                        for kc in range(8):
                            i = t_.matmul(
                                bank[:, 0:512],
                                A[:, AXN + kc * rows + t * 128:
                                  AXN + kc * rows + (t + 1) * 128],
                                A[:, AW + kc * D + half * 512:
                                  AW + kc * D + (half + 1) * 512],
                                start=(kc == 0), stop=(kc == 7))
                    k.inc(i, "sP")
                self.block(tensor=tP)

                if epi == "l2":
                    def aP(a, bank0=bank0, bank1=bank1):
                        k.wait(a, "sP")
                        k.wait(a, "sV")
                        a.activation(sq[:, 0:512], bank0[:, 0:512], AF.Square)
                        k.inc(a.activation(sq[:, 512:1024], bank1[:, 0:512],
                                           AF.Square), "sA")
                    self.block(scalar=aP)

                    def vP(v, bank0=bank0, bank1=bank1, obuf=obuf):
                        k.wait(v, "sA")
                        k.inc(v.reduce_sum(
                            bass.AP(FA, 1024, [[FA_SZ, 128], [1, 16], [1, 1]]),
                            sq.rearrange("p (h e) -> p h e", h=16), axis=AX.X),
                            "sV")
                    self.block(vector=vP)

                    def aP2(a):
                        k.wait(a, "sV")
                        k.inc(a.activation(sn[:], ss[:], AF.Sqrt), "sA")
                    self.block(scalar=aP2)

                    def vP2(v, bank0=bank0, bank1=bank1, obuf=obuf):
                        k.wait(v, "sA")
                        k.inc(v.tensor_scalar_max(rn, sn, 1e-12), "sV")
                        k.wait(v, "sV")
                        k.inc(v.reciprocal(sn, rn), "sV")
                        k.wait(v, "sV")
                        rep0 = bass.AP(FA, 1040, [[FA_SZ, 128], [1, 8], [0, 64]])
                        rep1 = bass.AP(FA, 1048, [[FA_SZ, 128], [1, 8], [0, 64]])
                        v.tensor_tensor(
                            obuf[:, 0:512].rearrange("p (h e) -> p h e", h=8),
                            bass.AP(bank0, 0, [[512, 128], [64, 8], [1, 64]]),
                            rep0, ALU.mult)
                        k.inc(v.tensor_tensor(
                            obuf[:, 512:1024].rearrange("p (h e) -> p h e", h=8),
                            bass.AP(bank1, 0, [[512, 128], [64, 8], [1, 64]]),
                            rep1, ALU.mult), "sV")
                    self.block(vector=vP2)
                elif epi == "copy":
                    def vP3(v, bank0=bank0, bank1=bank1, obuf=obuf):
                        k.wait(v, "sP")
                        v.tensor_copy(obuf[:, 0:512], bank0[:, 0:512])
                        k.inc(v.tensor_copy(obuf[:, 512:1024], bank1[:, 0:512]),
                              "sV")
                    self.block(vector=vP3)
                else:
                    def aP4(a, bank0=bank0, bank1=bank1, obuf=obuf):
                        k.wait(a, "sP")
                        a.activation(obuf[:, 0:512], bank0[:, 0:512], AF.Silu)
                        k.inc(a.activation(obuf[:, 512:1024], bank1[:, 0:512],
                                           AF.Silu), "sA")
                    self.block(scalar=aP4)

                    def vP4(v):
                        k.wait(v, "sA")
                        k.inc(v.engine_nop(), "sV")
                    self.block(vector=vP4)
                vprev[0] = self.cnt("sV")

                def gP(g, t=t, obuf=obuf, dest=dest):
                    k.wait(g, "sV")
                    k.inc(g.dma_start(
                        bass.AP(dest, t * 128 * D, [[D, 128], [1, D]]),
                        obuf), "sG", 16)
                self.block(gpsimd=gP)
            self.barrier()

        def sAB(s):
            for kc in range(8):
                k.inc(s.dma_start(
                    A[:, AWAB + kc * 32:AWAB + (kc + 1) * 32],
                    bass.AP(self.wbf["wab"], kc * 128 * 32, [[32, 128], [1, 32]])),
                    "sL", 16)
        self.block(sync=sAB)
        o_ba = self.BR["ba"][0]
        zt = FA[:, 0:32]
        t1 = FA[:, 32:64]
        t1b = FA[:, 96:128]
        t2 = FA[:, 64:96]
        for j, t in enumerate(range(NT)):
            def tAB(t_, t=t, j=j):
                if j == 0:
                    k.wait(t_, "sL")
                if j >= 1:
                    k.wait(t_, "sV", self._ab_v[j - 1])
                for kc in range(8):
                    t_.matmul(k.pE[:, 0:32],
                              A[:, AXN + kc * rows + t * 128:
                                AXN + kc * rows + (t + 1) * 128],
                              A[:, AWAB + kc * 32:AWAB + (kc + 1) * 32],
                              start=(kc == 0), stop=False)
                k.inc(t_.matmul(k.pE[:, 0:32], k.ones_row[0:1, :],
                                k.biasrows[0:1, o_ba:o_ba + 32],
                                start=False, stop=True), "sP")
            self.block(tensor=tAB)

            def vAB(v, t=t):
                k.wait(v, "sP")
                k.wait(v, "sA")
                k.inc(v.tensor_copy(zt[:], k.pE[:, 0:32]), "sV")
            self.block(vector=vAB)

            def aAB(a, t=t):
                k.wait(a, "sV")
                a.activation(k.ab_tm[:, t * 48 + 32:t * 48 + 48],
                             zt[:, 16:32], AF.Sigmoid)
                a.activation(t1[:], zt[:], AF.Abs)
                a.activation(t1b[:], t1[:], AF.Exp, scale=-1.0)
                k.inc(a.activation(t1[:], t1b[:], AF.Ln, bias=1.0), "sA")
            self.block(scalar=aAB)

            def vAB2(v, t=t, j=j):
                k.wait(v, "sA")
                k.inc(v.tensor_scalar_min(t2[:], zt[:], 0.0), "sV")
                k.wait(v, "sV")
                k.inc(v.tensor_tensor(k.ab_tm[:, t * 48:t * 48 + 32],
                                      t2[:], t1[:], ALU.subtract), "sV")
            if j == 0:
                self._ab_v = []
            self.block(vector=vAB2)
            self._ab_v.append(self.cnt("sV"))
            if j == 0 and "dbgab" in self.taps and "dbgab" not in self.tap_handles:
                def emit_dbgab2(hnd):
                    self.block(gpsimd=lambda g: self.inc(g.dma_start(
                        bass.AP(hnd, 0, [[320, 128], [1, 320]]),
                        FA[:, 0:320]), "sG", 16))
                self.tapout("dbgab", emit_dbgab2)
            if getattr(self, "_paranoid_ab", False):
                self.barrier()
        self.barrier()

        def vHG(v):
            for t in range(HT):
                i = v.tensor_scalar(k.ab_tm[:, t * 48 + 32:t * 48 + 48],
                                    k.ab_tm[:, t * 48 + 32:t * 48 + 48],
                                    k.gate_col[:, 0:1], None, ALU.mult)
            k.inc(i, "sV")
        self.block(vector=vHG)
        self.barrier()

    def attn_lg(self):
        k = self
        FA = self.farena
        la_f = FA[0:16, 2048:2048 + 128]
        lg_f = FA[0:16, 2176:2176 + 128]
        lgc = self.fmrow[0:16, 0:NT]
        for c in range(NT):
            def tL(t_, c=c):
                if c >= 1:
                    k.wait(t_, "sV", self._lg_v[c - 1])
                k.inc(t_.transpose(k.pE[0:16, 0:128],
                                   k.ab_tm[:, c * 48:c * 48 + 16],
                                   k.ident_f[:, 0:128]), "sP")
            self.block(tensor=tL)

            def vL(v, c=c):
                k.wait(v, "sP")
                k.inc(v.tensor_copy(la_f[:], k.pE[0:16, 0:128]), "sV")
                k.wait(v, "sV")
                k.inc(v.tensor_tensor_scan(lg_f[:], la_f[:], la_f[:], 0.0,
                                           ALU.add, ALU.bypass), "sV")
                k.wait(v, "sV")
                k.inc(v.tensor_copy(lgc[:, c:c + 1], lg_f[:, 127:128]), "sV")
            self.block(vector=vL)

            def tL2(t_, c=c):
                k.wait(t_, "sV")
                k.inc(t_.transpose(k.pF[:, 0:16], lg_f[:],
                                   k.ident_f[0:16, 0:16]), "sP")
            self.block(tensor=tL2)

            def vL2(v, c=c):
                k.wait(v, "sP")
                k.inc(v.tensor_copy(k.lg_tm[:, c * 16:(c + 1) * 16],
                                    k.pF[:, 0:16]), "sV")
            if c == 0:
                self._lg_v = []
            self.block(vector=vL2)
            self._lg_v.append(self.cnt("sV"))

        def tL3(t_):
            k.wait(t_, "sV")
            k.inc(t_.transpose(k.pE[0:NT, 0:16], lgc[:, 0:NT],
                               k.ident_f[0:16, 0:16]), "sP")
        self.block(tensor=tL3)

        def vL3(v):
            k.wait(v, "sP")
            k.inc(v.tensor_copy(FA[0:NT, 2048:2064], k.pE[0:NT, 0:16]), "sV")
        self.block(vector=vL3)

        def gL(g):
            k.wait(g, "sV")
            k.inc(g.dma_start(self.lgc_dram.ap(), FA[0:NT, 2048:2064]), "sG", 16)
            k.wait(g, "sG")
            k.inc(g.dma_start(k.lgc_flat[0:1, 0:NT * NH],
                              bass.AP(self.lgc_dram, 0, [[NT * NH, 1], [1, NT * NH]])),
                  "sG", 16)
        self.block(gpsimd=gL)
        self.barrier()

    def attn_scan(self):
        k = self
        A, FA = self.arena, self.farena
        rows = TLOC
        SC_KF = 0
        SC_QF = SC_KF + 2 * 1024
        SC_KT = SC_QF + 2 * 1024
        SC_VT = SC_KT + 2 * 1024
        SC_PT = SC_VT + 2 * 1024
        SC_NT_ = SC_PT + 16 * 128
        SC_NN = SC_NT_ + 16 * 128
        SC_P2 = SC_NN + 16 * 128
        SC_P2T = SC_P2 + 16 * 128
        SC_P4T = SC_P2T + 16 * 128
        SC_R = SC_P4T + 16 * 128
        SC_U1 = SC_R + 1024
        SC_U2 = SC_U1 + 1024
        SC_U3 = SC_U2 + 1024
        SC_KPP = SC_U3 + 1024
        SC_GT = SC_KPP + 1024
        SC_YF = SC_GT + 2 * 1024
        assert SC_YF + 1024 <= AR_SZ
        FY = 0
        FD = 1024
        FRT = 1280
        FCOL = 1536
        FBV = 2048
        FSQ = 3072
        FST = 4160
        FGN = 4224
        assert FGN + 1024 <= FA_SZ

        o_gn = self.BR["gnorm"][0]

        def tG(t_):
            for half in range(2):
                k.inc(t_.matmul(k.pE[:, 0:512] if half == 0 else k.pF[:, 0:512],
                                k.ones_row[0:1, :],
                                k.biasrows[0:1, o_gn + half * 512:
                                           o_gn + (half + 1) * 512],
                                start=True, stop=True), "sP")
        self.block(tensor=tG)

        def vG_(v):
            k.wait(v, "sP")
            v.tensor_copy(FA[:, FGN:FGN + 512], k.pE[:, 0:512])
            k.inc(v.tensor_copy(FA[:, FGN + 512:FGN + 1024], k.pF[:, 0:512]),
                  "sV")
        self.block(vector=vG_)
        self.barrier()

        chend = []
        for c in range(NT):
            real = c >= HT
            par = c % 2
            kf = A[:, SC_KF + par * 1024:SC_KF + par * 1024 + 1024]
            qf = A[:, SC_QF + par * 1024:SC_QF + par * 1024 + 1024]
            kt = A[:, SC_KT + par * 1024:SC_KT + par * 1024 + 1024]
            vt = A[:, SC_VT + par * 1024:SC_VT + par * 1024 + 1024]
            gt = A[:, SC_GT + par * 1024:SC_GT + par * 1024 + 1024]

            def sS(s, c=c, real=real, kf=kf, qf=qf, kt=kt, vt=vt, gt=gt):
                if c >= 2:
                    k.wait(s, "sP", chend[c - 2])
                for kc in range(8):
                    k.inc(s.dma_start(
                        kf[:, kc * 128:(kc + 1) * 128],
                        bass.AP(self.k_dram, c * 128 * D + kc * 128,
                                [[D, 128], [1, 128]]), transpose=True), "sL", 16)
                k.inc(s.dma_start(
                    kt, bass.AP(self.k_dram, c * 128 * D, [[D, 128], [1, D]])),
                    "sL", 16)
                k.inc(s.dma_start(
                    vt, bass.AP(self.v_dram, c * 128 * D, [[D, 128], [1, D]])),
                    "sL", 16)
                if real:
                    for kc in range(8):
                        k.inc(s.dma_start(
                            qf[:, kc * 128:(kc + 1) * 128],
                            bass.AP(self.q_dram, c * 128 * D + kc * 128,
                                    [[D, 128], [1, 128]]), transpose=True),
                            "sL", 16)
                    k.inc(s.dma_start(
                        gt, bass.AP(self.gate_dram, c * 128 * D, [[D, 128], [1, D]])),
                        "sL", 16)
            self.block(sync=sS)
            sl_done = self.cnt("sL")

            lgc_ = k.lg_tm[:, c * 16:(c + 1) * 16]
            lac_ = k.ab_tm[:, c * 48:c * 48 + 16]
            lbc_ = k.ab_tm[:, c * 48 + 16:c * 48 + 32]
            bsc_ = k.ab_tm[:, c * 48 + 32:c * 48 + 48]
            shb = FA[:, FCOL:FCOL + 16]
            negshbE = FA[:, FCOL + 16:FCOL + 32]
            explg = FA[:, FCOL + 32:FCOL + 48]
            neglg = FA[:, FCOL + 48:FCOL + 64]
            kscale = FA[:, FCOL + 64:FCOL + 80]
            explgC = FA[:, FCOL + 80:FCOL + 96]

            def tC0(t_, c=c):
                k.wait(t_, "sV")
                k.inc(t_.matmul(k.pG[:, 448:464], k.onesf_row[0:1, :],
                                k.lgc_flat[0:1, c * 16:(c + 1) * 16],
                                start=True, stop=True), "sP")
            self.block(tensor=tC0)

            def vC0(v, c=c):
                k.wait(v, "sP")
                k.inc(v.tensor_tensor(shb[:], lgc_, lac_, ALU.subtract), "sV")
                k.wait(v, "sV")
                k.inc(v.tensor_tensor(shb[:], shb[:], lbc_, ALU.add), "sV")
                v.tensor_scalar_mul(neglg[:], lgc_, -1.0)
                k.inc(v.tensor_tensor(FA[:, FCOL + 96:FCOL + 112],
                                      k.pG[:, 448:464], lgc_,
                                      ALU.subtract), "sV")
            self.block(vector=vC0)

            def aC0(a, c=c):
                k.wait(a, "sV")
                a.activation(explg[:], lgc_, AF.Exp)
                a.activation(kscale[:], FA[:, FCOL + 96:FCOL + 112], AF.Exp)
                a.activation(explgC[:], k.pG[:, 448:464], AF.Exp)
                k.inc(a.activation(FA[:, FCOL + 112:FCOL + 128], shb[:],
                                   AF.Exp), "sA")
            self.block(scalar=aC0)

            def vC1(v, c=c):
                k.wait(v, "sA")
                if c < HT:
                    i = v.tensor_scalar(negshbE[:], FA[:, FCOL + 112:FCOL + 128],
                                        k.gateneg_col[:, 0:1], None, ALU.mult)
                else:
                    i = v.tensor_scalar_mul(negshbE[:],
                                            FA[:, FCOL + 112:FCOL + 128], -1.0)
                k.inc(i, "sV")
            self.block(vector=vC1)

            def gC(g, c=c, kt=kt, vt=vt):
                k.wait(g, "sL", sl_done)
                k.wait(g, "sV")
                bsr = bass.AP(k.ab_tm, c * 48 + 32, [[NT * 48, 128], [1, 16], [0, 64]])
                g.tensor_tensor(FA[:, FBV:FBV + 1024].rearrange(
                    "p (h e) -> p h e", h=16), vt.rearrange(
                    "p (h e) -> p h e", h=16), bsr, ALU.mult)
                ksr = bass.AP(FA, FCOL + 64, [[FA_SZ, 128], [1, 16], [0, 64]])
                k.inc(g.tensor_tensor(A[:, SC_KPP:SC_KPP + 1024].rearrange(
                    "p (h e) -> p h e", h=16), kt.rearrange(
                    "p (h e) -> p h e", h=16), ksr, ALU.mult), "sO")
            self.block(gpsimd=gC)

            for h in range(NH):
                hp = h % 2
                hq = h // 2
                base = hp * 64
                kfs = kf[base:base + 128 - 64, hq * 128:(hq + 1) * 128]                     if False else bass.AP(kf, hq * 128, [[1024, 64], [1, 128]],
                                          ) if False else None
                kfs = kf[base:base + 64, hq * 128:(hq + 1) * 128]
                qfs = qf[base:base + 64, hq * 128:(hq + 1) * 128]
                Ms = k.M_bf[base:base + 64, hq * 64:(hq + 1) * 64]
                Mfs = k.M_all[base:base + 64, hq * 64:(hq + 1) * 64]
                pS = [k.pA, k.pB][hp]
                pD_ = [k.pC, k.pD][hp]
                pSo = [k.pE, k.pF][hp]
                Pt = A[:, SC_PT + h * 128:SC_PT + (h + 1) * 128]
                Nt = A[:, SC_NT_ + h * 128:SC_NT_ + (h + 1) * 128]
                Nn = A[:, SC_NN + h * 128:SC_NN + (h + 1) * 128]
                P2 = A[:, SC_P2 + h * 128:SC_P2 + (h + 1) * 128]
                P2t = A[:, SC_P2T + h * 128:SC_P2T + (h + 1) * 128]
                P4t = A[:, SC_P4T + h * 128:SC_P4T + (h + 1) * 128]
                Rs = A[:, SC_R + h * 64:SC_R + (h + 1) * 64]
                U1 = A[:, SC_U1 + h * 64:SC_U1 + (h + 1) * 64]
                U2 = A[:, SC_U2 + h * 64:SC_U2 + (h + 1) * 64]
                U3 = A[:, SC_U3 + h * 64:SC_U3 + (h + 1) * 64]
                Kpp = A[:, SC_KPP + h * 64:SC_KPP + (h + 1) * 64]
                rt1 = FA[:, FRT:FRT + 128]
                rt2 = FA[:, FRT + 128:FRT + 256]
                eD1 = FA[:, FD:FD + 128]
                eD2 = FA[:, FD + 128:FD + 256]

                def vR(v, h=h, real=real):
                    if h == 0:
                        k.wait(v, "sO")
                    k.wait(v, "sA")
                    i = v.tensor_scalar(rt2[:], k.mask_strict[:],
                                        shb[:, h:h + 1], None, ALU.add)
                    if real:
                        i = v.tensor_scalar(rt1[:], k.mask_incl[:],
                                            lgc_[:, h:h + 1], None, ALU.add)
                    k.inc(i, "sV")

                def tS(t_, h=h, real=real, kfs=kfs, qfs=qfs):
                    if h == 0:
                        k.wait(t_, "sL", sl_done)
                    k.wait(t_, "sV")
                    if h >= 2:
                        k.wait(t_, "sV", self._sc_sv[h - 2])
                    t_.matmul(pS[:, 0:128], kfs, kfs, start=True, stop=True)
                    if real:
                        t_.matmul(pS[:, 128:256], kfs, qfs, start=True, stop=True)
                    t_.transpose(pD_[:, 128:256], rt2[:], k.ident_f[:])
                    if real:
                        t_.transpose(pD_[:, 0:128], rt1[:], k.ident_f[:])
                    k.inc(t_.nop(), "sP")
                self.block(vector=vR, tensor=tS)

                def aD(a, h=h, real=real):
                    k.wait(a, "sP")
                    i = a.activation(eD2[:], pD_[:, 128:256], AF.Exp,
                                     bias=neglg[:, h:h + 1])
                    if real:
                        i = a.activation(eD1[:], pD_[:, 0:128], AF.Exp,
                                         bias=neglg[:, h:h + 1])
                    k.inc(i, "sA")

                def vD_(v, h=h, real=real):
                    k.wait(v, "sA")
                    i = v.scalar_tensor_tensor(Nt, pS[:, 0:128], -1.0, eD2[:],
                                               ALU.mult, ALU.mult)
                    if real:
                        i = v.scalar_tensor_tensor(Pt, pS[:, 128:256], 1.0,
                                                   eD1[:], ALU.mult, ALU.mult)
                    k.inc(i, "sV")
                self.block(scalar=aD, vector=vD_)
                if h == 0:
                    self._sc_sv = []
                self._sc_sv.append(self.cnt("sV"))

                def tN(t_, h=h):
                    k.wait(t_, "sV")
                    k.inc(t_.transpose(k.pH[:, 0:128], Nt, k.ident_b[:]), "sP")

                def vN_(v, h=h):
                    k.wait(v, "sP")
                    k.inc(v.tensor_copy(Nn, k.pH[:, 0:128]), "sV")
                self.block(tensor=tN, vector=vN_)

                def tQ(t_, h=h):
                    k.wait(t_, "sV")
                    t_.matmul(pSo[:, 0:128], Nt, Nn, start=True, stop=True)
                    k.inc(t_.matmul(pSo[:, 128:256], Nn, Nt, start=True,
                                    stop=True), "sP")

                def vQ(v, h=h):
                    k.wait(v, "sP")
                    v.tensor_copy(P2, pSo[:, 0:128])
                    k.inc(v.tensor_copy(P2t, pSo[:, 128:256]), "sV")
                self.block(tensor=tQ, vector=vQ)

                def tQ2(t_, h=h):
                    k.wait(t_, "sV")
                    k.inc(t_.matmul(pSo[:, 256:384], P2, P2t, start=True,
                                    stop=True), "sP")

                def vQ2(v, h=h):
                    k.wait(v, "sP")
                    k.inc(v.tensor_copy(P4t, pSo[:, 256:384]), "sV")
                self.block(tensor=tQ2, vector=vQ2)

                def tM(t_, h=h, real=real, kfs=kfs, qfs=qfs, Ms=Ms):
                    k.wait(t_, "sV")
                    t_.matmul(k.pG[:, 0:64], kfs, Ms, start=True, stop=True)
                    if real:
                        t_.matmul(k.pG[:, 64:128], qfs, Ms, start=True,
                                  stop=True)
                    k.inc(t_.nop(), "sP")

                def vRB(v, h=h):
                    k.wait(v, "sP")
                    k.inc(v.scalar_tensor_tensor(
                        Rs, k.pG[:, 0:64], negshbE[:, h:h + 1],
                        FA[:, FBV + h * 64:FBV + (h + 1) * 64],
                        ALU.mult, ALU.add), "sV")
                self.block(tensor=tM, vector=vRB)

                def tA1(t_, h=h):
                    k.wait(t_, "sV")
                    k.inc(t_.matmul(k.pG[:, 128:192], Nt, Rs, start=True,
                                    stop=True), "sP")

                def vA1(v, h=h):
                    k.wait(v, "sP")
                    k.inc(v.tensor_tensor(U1, k.pG[:, 128:192], Rs, ALU.add),
                          "sV")
                self.block(tensor=tA1, vector=vA1)

                def tA2(t_, h=h):
                    k.wait(t_, "sV")
                    k.inc(t_.matmul(k.pG[:, 192:256], P2t, U1, start=True,
                                    stop=True), "sP")

                def vA2(v, h=h):
                    k.wait(v, "sP")
                    k.inc(v.tensor_tensor(U2, k.pG[:, 192:256], U1, ALU.add),
                          "sV")
                self.block(tensor=tA2, vector=vA2)

                def tA3(t_, h=h):
                    k.wait(t_, "sV")
                    k.inc(t_.matmul(k.pG[:, 256:320], P4t, U2, start=True,
                                    stop=True), "sP")

                def vA3(v, h=h):
                    k.wait(v, "sP")
                    k.inc(v.tensor_tensor(U3, k.pG[:, 256:320], U2, ALU.add),
                          "sV")
                self.block(tensor=tA3, vector=vA3)

                if real:
                    def tO(t_, h=h, qfs=qfs, Ms=Ms):
                        k.wait(t_, "sV")
                        t_.matmul(k.pG[:, 320:384], Pt, U3, start=True,
                                  stop=True)
                        k.inc(t_.matmul(k.pG[:, 384:448], qfs, Ms, start=True,
                                        stop=True), "sP")

                    def vO(v, h=h):
                        k.wait(v, "sP")
                        k.inc(v.tensor_scalar(FA[:, FD:FD + 64],
                                              k.pG[:, 384:448],
                                              explg[:, h:h + 1], None,
                                              ALU.mult), "sV")
                        k.wait(v, "sV")
                        k.inc(v.tensor_tensor(
                            FA[:, FY + h * 64:FY + (h + 1) * 64],
                            FA[:, FD:FD + 64], k.pG[:, 320:384], ALU.add),
                            "sV")
                    self.block(tensor=tO, vector=vO)

                def tU(t_, h=h, hp=hp, Kpp=Kpp):
                    k.wait(t_, "sV")
                    k.inc(t_.matmul(k.pG[base:base + 64, 448:512], Kpp, U3,
                                    start=True, stop=True,
                                    tile_position=(0, base)), "sP")

                def vU(v, h=h, base=base, Mfs=Mfs, Ms=Ms):
                    k.wait(v, "sP")
                    k.inc(v.scalar_tensor_tensor(
                        Mfs, Mfs, explgC[base:base + 64, h:h + 1],
                        k.pG[base:base + 64, 448:512], ALU.mult, ALU.add), "sV")
                    k.wait(v, "sV")
                    k.inc(v.tensor_copy(Ms, Mfs), "sV")
                self.block(tensor=tU, vector=vU)
            chend.append(self.cnt("sP"))

            if real:
                ssy = FA[:, FST:FST + 1]
                rsy = FA[:, FST + 1:FST + 2]

                def aY(a, c=c):
                    k.wait(a, "sV")
                    k.inc(a.activation(FA[:, FSQ:FSQ + 1024],
                                       FA[:, FY:FY + 1024], AF.Square,
                                       accum_out=ssy[:, 0:1]), "sA")
                self.block(scalar=aY)

                def vY(v, c=c):
                    k.wait(v, "sA")
                    k.inc(v.tensor_scalar_mul(rsy[:], ssy[:], 1.0 / D), "sV")
                self.block(vector=vY)

                def aY2(a, c=c):
                    k.wait(a, "sV")
                    a.activation(FA[:, FST + 2:FST + 3], rsy[:], AF.Ln,
                                 bias=LN_EPS)
                    k.inc(a.activation(rsy[:], FA[:, FST + 2:FST + 3], AF.Exp,
                                       scale=-0.5), "sA")
                self.block(scalar=aY2)

                def vY2(v, c=c, gt=gt, par=par):
                    k.wait(v, "sA")
                    k.wait(v, "sG")
                    k.wait(v, "sL", sl_done)
                    v.tensor_tensor(FA[:, FSQ:FSQ + 1024], gt,
                                    FA[:, FGN:FGN + 1024], ALU.mult)
                    k.inc(v.scalar_tensor_tensor(
                        A[:, SC_YF:SC_YF + 1024], FA[:, FY:FY + 1024],
                        rsy[:, 0:1], FA[:, FSQ:FSQ + 1024],
                        ALU.mult, ALU.mult), "sV")
                self.block(vector=vY2)

                def gY(g, c=c):
                    k.wait(g, "sV")
                    k.inc(g.dma_start(
                        bass.AP(self.yf_dram, c * 128 * D, [[D, 128], [1, D]]),
                        A[:, SC_YF:SC_YF + 1024]), "sG", 16)
                self.block(gpsimd=gY)
        self.barrier()

    def attn_wo(self):
        k = self
        A = self.arena
        rows = TLOC
        YF = 0
        WO = YF + 8 * rows
        assert WO + 8 * D <= AR_SZ

        def s1(s):
            for kc in range(8):
                k.inc(s.dma_start(
                    A[:, YF + kc * rows:YF + (kc + 1) * rows],
                    bass.AP(self.yf_dram, kc * 128, [[D, rows], [1, 128]]),
                    transpose=True), "sL", 16)
            for kc in range(8):
                k.inc(s.dma_start(
                    A[:, WO + kc * D:WO + (kc + 1) * D],
                    bass.AP(self.wbf["woa"], kc * 128 * D, [[D, 128], [1, D]])),
                    "sL", 16)
        self.block(sync=s1)

        v0 = [None]
        for j, t in enumerate(range(HT, NT)):
            bank0, bank1 = (k.pA, k.pB) if j % 2 == 0 else (k.pC, k.pD)

            def tN(t_, t=t, j=j, bank0=bank0, bank1=bank1):
                if j == 0:
                    k.wait(t_, "sL")
                if j >= 2:
                    k.wait(t_, "sV", v0[0] + (j - 1))
                for half, bank in ((0, bank0), (1, bank1)):
                    for kc in range(8):
                        i = t_.matmul(
                            bank[:, 0:512],
                            A[:, YF + kc * rows + t * 128:
                              YF + kc * rows + (t + 1) * 128],
                            A[:, WO + kc * D + half * 512:
                              WO + kc * D + (half + 1) * 512],
                            start=(kc == 0), stop=(kc == 7))
                k.inc(i, "sP")
            self.block(tensor=tN)

            def vN(v, t=t, bank0=bank0, bank1=bank1):
                k.wait(v, "sP")
                v.tensor_tensor(k.x_tm[:, t * D:t * D + 512],
                                k.x_tm[:, t * D:t * D + 512],
                                bank0[:, 0:512], ALU.add)
                k.inc(v.tensor_tensor(k.x_tm[:, t * D + 512:(t + 1) * D],
                                      k.x_tm[:, t * D + 512:(t + 1) * D],
                                      bank1[:, 0:512], ALU.add), "sV")
            if j == 0:
                v0[0] = self.cnt("sV")
            self.block(vector=vN)
        self.barrier()

    def finish(self):
        k = self

        def g(g_):
            for t in range(HT, NT):
                k.inc(g_.dma_start(
                    bass.AP(self.out, (t - HT) * 128 * D, [[D, 128], [1, D]]),
                    k.x_tm[:, t * D:(t + 1) * D]), "sG", 16)
            k.wait(g_, "sG")
        self.block(gpsimd=g)

    def build(self):
        return self.nc, self.tap_handles


def _build(taps=()):
    kb = KB(taps)
    kb.boot()
    kb.gh_precompute()
    kb.ln(list(range(NT)))
    kb.tap_dram("xn1", kb.xn_dram)
    kb.ffn(list(range(NT)), tag=1)
    kb.tap_xtm("x1", list(range(NT)))
    kb.ln(list(range(NT)))
    kb.attn_proj()
    kb.tap_dram("qn", kb.q_dram)
    kb.tap_dram("kn", kb.k_dram)
    kb.tap_dram("vv", kb.v_dram)
    if "abtm" in kb.taps:
        kb.tapout("abtm", lambda hnd: kb.block(gpsimd=lambda g: kb.inc(
            g.dma_start(bass.AP(hnd, 0, [[NT * 48, 128], [1, NT * 48]]),
                        kb.ab_tm[:]), "sG", 16)))
    kb.attn_lg()
    if "lgtm" in kb.taps:
        kb.tapout("lgtm", lambda hnd: kb.block(gpsimd=lambda g: kb.inc(
            g.dma_start(bass.AP(hnd, 0, [[NT * NH, 128], [1, NT * NH]]),
                        kb.lg_tm[:]), "sG", 16)))
    kb.attn_scan()
    kb.tap_dram("yf", kb.yf_dram)
    kb.attn_wo()
    kb.tap_xtm("x2", list(range(NT)))
    kb.ln(list(range(HT, NT)))
    kb.ffn(list(range(HT, NT)), tag=2)
    kb.finish()
    return kb.build()


_CACHE = {}


def _make_inmaps(inputs):
    x = np.asarray(inputs["L0_attn_x"], np.float32)
    w = {
        "Wq": inputs["L0_attn_Wq"], "Wk": inputs["L0_attn_Wk"],
        "Wv": inputs["L0_attn_Wv"], "Wa": inputs["L0_attn_Wa"],
        "Wb": inputs["L0_attn_Wb"], "Wg": inputs["L0_attn_Wg"],
        "Wo": inputs["L0_attn_Wo"],
        "ba": np.reshape(inputs["L0_attn_ba"], (1, -1)),
        "bb": np.reshape(inputs["L0_attn_bb"], (1, -1)),
        "g_norm": np.reshape(inputs["L0_attn_g_norm"], (1, -1)),
        "slot_init": inputs["L0_ffn_slot_init"],
        "slot_scale": inputs["L0_ffn_slot_scale"],
        "w1": inputs["L0_ffn_w1"],
        "b1": np.reshape(inputs["L0_ffn_b1"], (1, -1)),
        "w2": inputs["L0_ffn_w2"],
        "b2": np.reshape(inputs["L0_ffn_b2"], (1, -1)),
        "gru_wih": inputs["L0_ffn_gru_wih"],
        "gru_whh": inputs["L0_ffn_gru_whh"],
        "gru_bih": np.reshape(inputs["L0_ffn_gru_bih"], (1, -1)),
        "gru_bhh": np.reshape(inputs["L0_ffn_gru_bhh"], (1, -1)),
        "wv": inputs["L0_ffn_wv"],
        "bv": np.reshape(inputs["L0_ffn_bv"], (1, -1)),
        "wo": inputs["L0_ffn_wo"],
        "bo": np.reshape(inputs["L0_ffn_bo"], (1, -1)),
        "tau": np.reshape(inputs["L0_ffn_tau"], (1, 1)),
    }
    w = {nm: np.ascontiguousarray(np.asarray(a, np.float32)) for nm, a in w.items()}
    in_maps = []
    for c in range(8):
        b, s = c // 4, c % 4
        t0 = s * 2048
        if s == 0:
            xs = np.concatenate([x[b, 0:384], x[b, 0:2048]], axis=0)
            hg = 0.0
        else:
            xs = x[b, t0 - 384:t0 + 2048]
            hg = 1.0
        m = dict(w)
        m["x_slice"] = np.ascontiguousarray(xs)
        m["halo_gate"] = np.full((128, 1), hg, np.float32)
        in_maps.append(m)
    return in_maps


def kernel(**inputs):
    if "prog" not in _CACHE:
        _CACHE["prog"] = _build()
    nc, _ = _CACHE["prog"]
    in_maps = _make_inmaps(inputs)
    res = run_bass_kernel_spmd(nc, in_maps, core_ids=list(range(8)))
    out = np.empty((2, 8192, 1024), np.float32)
    for c in range(8):
        b, s = c // 4, c % 4
        out[b, s * 2048:(s + 1) * 2048] = res.results[c]["out"]
    return out
